# version 19
# baseline (speedup 1.0000x reference)
import contextlib
import numpy as np
import ml_dtypes
import concourse.bass as bass
import concourse.mybir as mybir
from concourse.bass_utils import run_bass_kernel_spmd

F32 = mybir.dt.float32
BF16 = mybir.dt.bfloat16
AF = mybir.ActivationFunctionType
OP = mybir.AluOpType

P = 128
CH = 8
T = 2048
NTB = 2
NQB = 4
KT = 16
RS = 0.08838834764831845
EPS_A = 1e-5
EPS_F = 1e-6

NODES = [
    (0, -1, 0, -1, 0, -1),
    (2, 0, 1, -1, 2, 0),
    (5, 1, 3, -1, 4, 0),
    (9, 2, 6, -1, 8, 1),
    (14, 3, 10, -1, 12, 1),
    (19, 4, 15, 0, 17, 2),
]


_COMPUTE_OPS = {
    "tensor_copy", "tensor_scalar", "tensor_tensor", "scalar_tensor_tensor",
    "tensor_reduce", "reduce_max", "reduce_sum", "reciprocal", "memset",
    "activation", "copy", "mul", "add", "sqrt", "square", "iota", "select",
    "tensor_scalar_mul", "tensor_scalar_add", "tensor_add", "tensor_sub",
    "tensor_mul", "tensor_max", "reciprocal_approx_accurate",
    "reciprocal_approx_fast",
}


class EngWrap:

    def __init__(self, eng, autosem=None):
        object.__setattr__(self, "_e", eng)
        object.__setattr__(self, "last", None)
        object.__setattr__(self, "auto", autosem)

    def __getattr__(self, n):
        f = getattr(self._e, n)
        if callable(f):
            def g(*a, **k):
                auto = self.auto
                if auto is not None and n in _COMPUTE_OPS and auto.n > 0:
                    self._e.wait_ge(auto.h, auto.n)
                r = f(*a, **k)
                if hasattr(r, "then_inc"):
                    object.__setattr__(self, "last", r)
                    if auto is not None and n in _COMPUTE_OPS:
                        r.then_inc(auto.h, 1)
                        auto.n += 1
                return r
            return g
        return f


class Sem:
    def __init__(self, nc, ctx, name):
        self.h = ctx.enter_context(nc.semaphore(name))
        self.n = 0

    def inc(self, inst, k=1):
        inst.then_inc(self.h, k)
        self.n += k
        return self.n

    def wait(self, eng, val=None):
        v = self.n if val is None else val
        assert v <= self.n, f"wait beyond produced count {v} > {self.n}"
        if v > 0:
            eng.wait_ge(self.h, v)


def build_nc(upto="full", dbg_node=None):
    nc = bass.Bass("TRN2", target_bir_lowering=False)

    x_in = nc.dram_tensor("x", [T, 1024], F32, kind="ExternalInput").ap()
    wq_a = nc.dram_tensor("attn_wq", [1024, 1024], F32, kind="ExternalInput").ap()
    wkd_a = nc.dram_tensor("attn_wk_down", [1024, 64], F32, kind="ExternalInput").ap()
    wvd_a = nc.dram_tensor("attn_wv_down", [1024, 64], F32, kind="ExternalInput").ap()
    wku_a = nc.dram_tensor("attn_wk_up", [32, 128], F32, kind="ExternalInput").ap()
    wvu_a = nc.dram_tensor("attn_wv_up", [32, 128], F32, kind="ExternalInput").ap()
    wo_a = nc.dram_tensor("attn_wo", [1024, 1024], F32, kind="ExternalInput").ap()
    ew_in = nc.dram_tensor("ffn_ew", [6, 3], F32, kind="ExternalInput").ap()
    aw_in = nc.dram_tensor("ffn_act_w", [6], F32, kind="ExternalInput").ap()
    We_in = nc.dram_tensor("ffn_We", [20, 1024, 1024], F32, kind="ExternalInput").ap()
    be_in = nc.dram_tensor("ffn_be", [20, 1024], F32, kind="ExternalInput").ap()
    Wq_in = nc.dram_tensor("ffn_Wq", [6, 1024, 1024], F32, kind="ExternalInput").ap()
    Wk_in = nc.dram_tensor("ffn_Wk", [6, 1024, 1024], F32, kind="ExternalInput").ap()
    Wv_in = nc.dram_tensor("ffn_Wv", [6, 1024, 1024], F32, kind="ExternalInput").ap()
    Wo_in = nc.dram_tensor("ffn_Wo", [6, 1024, 1024], F32, kind="ExternalInput").ap()
    lng_in = nc.dram_tensor("ffn_ln_g", [1024], F32, kind="ExternalInput").ap()
    lnb_in = nc.dram_tensor("ffn_ln_b", [1024], F32, kind="ExternalInput").ap()
    ident_f = nc.dram_tensor("ident_f", [128, 128], F32, kind="ExternalInput").ap()
    ones_f = nc.dram_tensor("ones_f", [128, 128], F32, kind="ExternalInput").ap()
    ones_b = nc.dram_tensor("ones_b", [128, 128], BF16, kind="ExternalInput").ap()
    mask_in = nc.dram_tensor("mask", [128, 4, 512], BF16, kind="ExternalInput").ap()

    out_t = nc.dram_tensor("out", [T, 1024], F32, kind="ExternalOutput").ap()
    dbg_t = None
    if dbg_node is not None:
        dbg_t = nc.dram_tensor("dbg", [1024, T], BF16, kind="ExternalOutput").ap()

    x1b_t = nc.dram_tensor("x1b", [1024, T], BF16, kind="Internal").ap()
    x1f_t = nc.dram_tensor("x1f", [1024, T], F32, kind="Internal").ap()
    qsp_t = nc.dram_tensor("qsp", [1024, T], BF16, kind="Internal").ap()
    nsc_t = [nc.dram_tensor(f"nsc{c}", [1024, T], BF16, kind="Internal").ap()
             for c in range(5)]

    def fmv(dram):
        return dram.rearrange("(c p) t -> p c t", p=128)

    node_src = {-1: x1b_t}
    for c in range(5):
        node_src[c] = nsc_t[c]

    ctx = contextlib.ExitStack()
    with ctx:
        E = ctx.enter_context
        A = E(nc.sbuf_tensor("A", [P, CH, T], BF16))
        B = E(nc.sbuf_tensor("B", [P, CH, T], BF16))
        Cv = E(nc.sbuf_tensor("Cv", [P, KT, 1024], BF16))
        Dm = E(nc.sbuf_tensor("Dm", [P, CH * T], BF16))
        Wb0 = E(nc.sbuf_tensor("Wb0", [P, CH, 1024], BF16))
        Wb1 = E(nc.sbuf_tensor("Wb1", [P, CH, 1024], BF16))
        Wb = [Wb0, Wb1]
        wst = E(nc.sbuf_tensor("wst", [P, 1024], F32))
        LNw = E(nc.sbuf_tensor("LNw", [P, CH, 512], F32))
        x1w = E(nc.sbuf_tensor("x1w", [P, CH, 512], F32))
        mask_sb = E(nc.sbuf_tensor("mask_sb", [P, 4, 512], BF16))
        id_f = E(nc.sbuf_tensor("id_f", [P, 128], F32))
        on_f = E(nc.sbuf_tensor("on_f", [P, 128], F32))
        on_b = E(nc.sbuf_tensor("on_b", [P, 128], BF16))
        be_fm = E(nc.sbuf_tensor("be_fm", [P, CH, 20], F32))
        gb_fm = E(nc.sbuf_tensor("gb_fm", [P, CH, 2], F32))
        sc_fm = E(nc.sbuf_tensor("sc_fm", [P, 44], F32))
        rowb = E(nc.sbuf_tensor("rowb", [P, 1024], F32))
        wkup = E(nc.sbuf_tensor("wkup", [32, 2, 128], BF16))
        negC = E(nc.sbuf_tensor("negC", [P, 8], F32))

        psA = E(nc.psum_tensor("psA", [P, 1024], F32))
        psS = E(nc.psum_tensor("psS", [P, 2048], F32))
        psY = E(nc.psum_tensor("psY", [P, 512], F32))
        psM = E(nc.psum_tensor("psM", [P, 512], F32))

        A3 = A.ap()
        B3 = B.ap()
        Dmf = Dm.ap()
        Dm3 = Dmf.rearrange("p (c t) -> p c t", c=CH)
        Dp = Dmf[:, 0:8192].rearrange("p (k q) -> p k q", k=16)
        Dc = Dmf[:, 8192:12288].rearrange("p (c q) -> p c q", c=CH)
        Dq = Dmf[:, 12288:16384].rearrange("p (c q) -> p c q", c=CH)
        Dn = Dmf[:, 0:4096].rearrange("p (c q) -> p c q", c=CH)
        Dsq = Dmf[:, 4096:8192].rearrange("p (c q) -> p c q", c=CH)
        x1wf = x1w.ap().rearrange("p c t -> p (c t)")
        LNwf = LNw.ap().rearrange("p c t -> p (c t)")
        rbc_sb = rowb.ap()[:, 512:1024]
        klat = B.ap()[0:32, 4:6, :]
        vlat = B.ap()[0:32, 6:8, :]
        khA = B.ap()[:, 0:2, :]

        dW = Sem(nc, ctx, "dW")
        dA = Sem(nc, ctx, "dA")
        dO = Sem(nc, ctx, "dO")
        dX = [Sem(nc, ctx, "dX0"), Sem(nc, ctx, "dX1")]
        dEa = [Sem(nc, ctx, "dEa0"), Sem(nc, ctx, "dEa1")]
        dEb = [Sem(nc, ctx, "dEb0"), Sem(nc, ctx, "dEb1")]
        dNkb = [Sem(nc, ctx, "dNkb0"), Sem(nc, ctx, "dNkb1")]
        dNk = [Sem(nc, ctx, "dNk0"), Sem(nc, ctx, "dNk1")]
        dQr = Sem(nc, ctx, "dQr")
        dXf = Sem(nc, ctx, "dXf")
        dSq = Sem(nc, ctx, "dSq")
        dSb = Sem(nc, ctx, "dSb")
        dSf = Sem(nc, ctx, "dSf")
        dSn = Sem(nc, ctx, "dSn")
        gPE = Sem(nc, ctx, "gPE")
        gEV = Sem(nc, ctx, "gEV")
        gSE = Sem(nc, ctx, "gSE")

        bank_free = {}
        a_free = [0]

        def bfree(key, eng):
            if key in bank_free:
                s, v = bank_free[key]
                s.wait(eng, v)

        TE, VE, SE, SY, GP = (EngWrap(nc.tensor), EngWrap(nc.vector, gEV),
                              EngWrap(nc.scalar, gSE), EngWrap(nc.sync), EngWrap(nc.gpsimd))

        def finish_all(eng=None):
            for sem, e in ((dW, SY), (dSq, SY), (dSb, SY), (dSf, SY), (dSn, SY),
                           (dA, GP), (dQr, GP), (dXf, GP), (dNk[0], GP), (dNk[1], GP),
                           (dNkb[0], GP), (dNkb[1], GP), (dEb[0], GP), (dEb[1], GP),
                           (dX[0], GP), (dX[1], GP), (dEa[0], GP), (dEa[1], GP),
                           (dO, SY)):
                if sem.n > 0:
                    sem.wait(e, sem.n)

        def dump_exit(src3, ncols=T):
            gPE.wait(SY); gEV.wait(SY); gSE.wait(SY); dW.wait(SY); 
            SY.dma_start(out=fmv(dbg_t)[:, :, 0:ncols], in_=src3).then_inc(dO.h, 16)
            dO.n += 16
            finish_all()

        for dst, src in ((id_f[:], ident_f), (on_f[:], ones_f), (on_b[:], ones_b),
                         (mask_sb[:], mask_in), (wst[0:20, :], be_in),
                         (rowb[32:33, 0:1024], lng_in[None, :]),
                         (rowb[64:65, 0:1024], lnb_in[None, :]),
                         (rowb[0:1, 0:18], ew_in.rearrange("a b -> (a b)")[None, :]),
                         (rowb[0:1, 18:24], aw_in[None, :])):
            SY.dma_start(out=dst, in_=src).then_inc(dW.h, 16)
            dW.n += 16
        init_dma = dW.n

        if upto == "init0":
            dW.wait(GP, init_dma)
            GP.dma_start(out=dbg_t[0:128, 0:128], in_=id_f[:]).then_inc(dO.h, 16)
            dO.n += 16
            dO.wait(GP, dO.n)
            finish_all()
            return nc
        dW.wait(TE, init_dma)
        for c in range(CH):
            dst = psM[:, c * 20:(c + 1) * 20] if c < 4 else psY[:, (c - 4) * 20:(c - 3) * 20]
            TE.transpose(out=dst, in_=wst[0:20, c * 128:(c + 1) * 128], identity=id_f[0:20, 0:20])
        for c in range(CH):
            TE.transpose(out=psS[:, c:c + 1], in_=rowb[32:33, c * 128:(c + 1) * 128],
                         identity=on_f[32:33, 0:1])
            TE.transpose(out=psS[:, 512 + c:513 + c], in_=rowb[64:65, c * 128:(c + 1) * 128],
                         identity=on_f[64:65, 0:1])
        TE.matmul(out=psS[:, 1024:1048], lhsT=on_f[0:1, :], rhs=rowb[0:1, 0:24],
                  start=True, stop=True)
        gPE.inc(TE.last)

        gPE.wait(VE)
        for c in range(CH):
            src = psM[:, c * 20:(c + 1) * 20] if c < 4 else psY[:, (c - 4) * 20:(c - 3) * 20]
            VE.tensor_copy(out=be_fm[:, c, :], in_=src)
            VE.tensor_copy(out=gb_fm[:, c, 0:1], in_=psS[:, c:c + 1])
            VE.tensor_copy(out=gb_fm[:, c, 1:2], in_=psS[:, 512 + c:513 + c])
        VE.tensor_copy(out=sc_fm[:, 0:24], in_=psS[:, 1024:1048])
        for c in range(6):
            VE.tensor_scalar(out=sc_fm[:, 24 + c:25 + c], in0=sc_fm[:, 3 * c:3 * c + 1],
                             scalar1=RS, scalar2=None, op0=OP.mult, op1=OP.bypass)
            VE.tensor_tensor(out=sc_fm[:, 30 + c:31 + c], in0=sc_fm[:, 3 * c + 2:3 * c + 3],
                             in1=sc_fm[:, 18 + c:19 + c], op=OP.mult)
        VE.memset(sc_fm[:, 42:43], 1e-30)
        VE.memset(sc_fm[:, 43:44], 1e-19)
        gEV.wait(SE)
        SE.activation(out=sc_fm[:, 36:42], in_=sc_fm[:, 30:36], func=AF.Ln,
                      bias=sc_fm[:, 42:43])
        for key in range(8):
            bank_free[key] = (gEV, gEV.n)

        gPE.wait(SY)
        SY.dma_start(out=wst[0:32, 0:128], in_=wku_a).then_inc(dW.h, 16); dW.n += 16
        SY.dma_start(out=wst[0:32, 128:256], in_=wvu_a).then_inc(dW.h, 16); dW.n += 16
        dW.wait(VE)
        VE.tensor_copy(out=wkup[:, 0, :], in_=wst[0:32, 0:128])
        VE.tensor_copy(out=wkup[:, 1, :], in_=wst[0:32, 128:256])
        wkup_ready = gEV.n
        wst_user = [(gEV, gEV.n)]

        if upto == "init":
            gEV.wait(GP)
            GP.dma_start(out=dbg_t[0:128, 0:160],
                         in_=be_fm.ap().rearrange("p c e -> p (c e)")).then_inc(dO.h, 16)
            dO.n += 16
            dO.wait(GP, dO.n)
            finish_all()
            return nc
        dWw = [Sem(nc, ctx, "dWw0"), Sem(nc, ctx, "dWw1")]
        wslots = []

        def add_slot(ap2d):
            wslots.append([(ap2d.rearrange("(c p) n -> p c n", p=128), 0, ap2d.shape[1])])

        def add_slot_pair(apa, apb):
            wslots.append([(apa.rearrange("(c p) n -> p c n", p=128), 0, 64),
                           (apb.rearrange("(c p) n -> p c n", p=128), 64, 64)])

        add_slot(wq_a)
        add_slot_pair(wkd_a, wvd_a)
        add_slot(wo_a)
        slot_order = []
        slot_scale = {}
        for c, (eq, inq, ek, ink, ev, inv) in enumerate(NODES):
            if c == 0:
                specs = [We_in[eq], Wv_in[c], Wq_in[c], Wk_in[c], Wo_in[c]]
            elif eq == ev and inq == inv:
                specs = [We_in[eq], Wv_in[c], Wq_in[c], We_in[ek], Wk_in[c], Wo_in[c]]
            else:
                specs = [We_in[ev], Wv_in[c], We_in[eq], Wq_in[c], We_in[ek],
                         Wk_in[c], Wo_in[c]]
            ords = []
            for ap2d in specs:
                ords.append(len(wslots))
                add_slot(ap2d)
            slot_order.append(ords)

        NSLOT = len(wslots)
        conv_done = [None] * NSLOT
        pass_pe_end = {}
        next_load = [0]

        def emit_wload(k):
            dst = Wb[k % 2]
            dep = pass_pe_end.get(k - 2, 0)
            if dep:
                gPE.wait(GP, dep)
            for v3, col0, ncol in wslots[k]:
                out = dst[:, :, col0:col0 + ncol] if ncol < 1024 else dst[:, :, :]
                GP.dma_start(out=out, in_=v3).then_inc(dWw[k % 2].h, 16)
                dWw[k % 2].n += 16
            conv_done[k] = (dWw[k % 2], dWw[k % 2].n)

        def use_slot(k):
            while next_load[0] <= min(k + 1, NSLOT - 1):
                emit_wload(next_load[0])
                next_load[0] += 1
            assert conv_done[k] is not None

        proj_g = [0]

        def proj_bank():
            bank = proj_g[0] % 4
            ps = (psA[:, bank * 512:(bank + 1) * 512] if bank < 2
                  else psS[:, (bank - 2) * 512:(bank - 1) * 512])
            return ps, bank

        def proj_groups(n_groups, mm_emit, evac_emit):
            for gi in range(n_groups):
                ps, bank = proj_bank()
                bfree(bank, TE)
                last = mm_emit(gi, ps)
                gPE.inc(last)
                pev = gPE.n
                sem_val = evac_emit(gi, ps, pev)
                bank_free[bank] = sem_val
                proj_g[0] += 1

        def proj_pass(wslot, rhs_fn, evac_fn, tbs=range(NTB), ns_waits=None):
            use_slot(wslot)
            W = Wb[wslot % 2]
            cv = conv_done[wslot]
            if ns_waits is None:
                items = [(tb, m, ns) for tb in tbs for m in range(CH) for ns in range(2)]
            else:
                items = [(tb, m, ns) for tb in tbs for ns in range(2) for m in range(CH)]

            def mm(gi, ps):
                tb, m, ns = items[gi]
                cv[0].wait(TE, cv[1])
                if ns_waits is not None:
                    ns_waits[ns][0].wait(TE, ns_waits[ns][1])
                for kc in range(CH):
                    inst = TE.matmul(out=ps, lhsT=W[:, kc, m * 128:(m + 1) * 128],
                                     rhs=rhs_fn(kc, tb, ns),
                                     start=(kc == 0), stop=(kc == CH - 1))
                return inst

            def ev(gi, ps, pev):
                tb, m, ns = items[gi]
                return evac_fn(m, tb, ns, ps, pev)
            proj_groups(len(items), mm, ev)
            pass_pe_end[wslot] = gPE.n

        def evac_plain_act(dst3, scale=None):
            def f(m, tb, ns, ps, pev):
                gPE.wait(SE, pev)
                col = tb * 1024 + ns * 512
                if scale is None:
                    SE.copy(out=dst3[:, m, col:col + 512], in_=ps)
                else:
                    SE.activation(out=dst3[:, m, col:col + 512], in_=ps,
                                  func=AF.Copy, scale=scale)
                return (gSE, gSE.n)
            return f

        def evac_bias_dve(dst3, eidx, colmap=None, first_waits=()):
            fw = [list(first_waits)]

            def f(m, tb, ns, ps, pev):
                gPE.wait(VE, pev)
                for sem, val in fw[0]:
                    sem.wait(VE, val)
                fw[0] = []
                col = (tb * 1024 + ns * 512) if colmap is None else colmap(tb, ns)
                VE.tensor_scalar(out=dst3[:, m, col:col + 512], in0=ps,
                                 scalar1=be_fm[:, m, eidx:eidx + 1], scalar2=None,
                                 op0=OP.add, op1=OP.bypass)
                return (gEV, gEV.n)
            return f

        def attn_norms(pairs_q, heads_k, qh3, kh3, pair_kv, extra_waits=()):
            qcol, kcol, ccol = 40, 56, 72
            items = [("q", i, qh3, qcol + i) for i in pairs_q] + \
                    [("k", i, kh3, kcol + i) for i in heads_k]
            sq_marks = [None] * len(items)
            mm_marks = [None] * len(items)
            first = True
            for idx, (side, i, src3, cell) in enumerate(items):
                half = idx % 2
                sq = Dmf[:, half * 2048:half * 2048 + 2048]
                gPE.wait(VE)
                gSE.wait(VE)
                if first:
                    for sm, v in extra_waits:
                        sm.wait(VE, v)
                    first = False
                VE.tensor_tensor(out=sq, in0=src3[:, i, :], in1=src3[:, i, :],
                                 op=OP.mult)
                sq_marks[idx] = gEV.n
                if idx >= 1:
                    j = idx - 1
                    pj = (j % 2)
                    sqj = Dmf[:, pj * 2048:pj * 2048 + 2048]
                    gEV.wait(TE, sq_marks[j])
                    for ns in range(4):
                        bfree(2 + ns, TE)
                        inst = TE.matmul(out=psS[0:1, ns * 512:ns * 512 + 512],
                                         lhsT=on_b[:, 0:1],
                                         rhs=sqj[:, ns * 512:(ns + 1) * 512],
                                         start=True, stop=True)
                        gPE.inc(inst)
                    mm_marks[j] = gPE.n
                    gPE.wait(VE, mm_marks[j])
                    VE.tensor_scalar(out=LNwf[0:1, 0:2048], in0=psS[0:1, 0:2048],
                                     scalar1=1.0, scalar2=None, op0=OP.mult,
                                     op1=OP.max,
                                     accum_out=rowb[0:1, items[j][3]:items[j][3] + 1])
                    for ns in range(4):
                        bank_free[2 + ns] = (gEV, gEV.n)
            j = len(items) - 1
            pj = j % 2
            sqj = Dmf[:, pj * 2048:pj * 2048 + 2048]
            gEV.wait(TE, sq_marks[j])
            for ns in range(4):
                bfree(2 + ns, TE)
                inst = TE.matmul(out=psS[0:1, ns * 512:ns * 512 + 512],
                                 lhsT=on_b[:, 0:1], rhs=sqj[:, ns * 512:(ns + 1) * 512],
                                 start=True, stop=True)
                gPE.inc(inst)
            gPE.wait(VE, gPE.n)
            VE.tensor_scalar(out=LNwf[0:1, 0:2048], in0=psS[0:1, 0:2048],
                             scalar1=1.0, scalar2=None, op0=OP.mult, op1=OP.max,
                             accum_out=rowb[0:1, items[j][3]:items[j][3] + 1])
            for ns in range(4):
                bank_free[2 + ns] = (gEV, gEV.n)
            for pair in pairs_q:
                kvh = pair_kv(pair)
                VE.tensor_tensor(out=rowb[0:1, ccol + pair:ccol + pair + 1],
                                 in0=rowb[0:1, qcol + pair:qcol + pair + 1],
                                 in1=rowb[0:1, kcol + kvh:kcol + kvh + 1], op=OP.mult)
            npair = len(pairs_q)
            gEV.wait(SE)
            SE.activation(out=rowb[0:1, ccol:ccol + npair],
                          in_=rowb[0:1, ccol:ccol + npair], func=AF.Ln,
                          bias=sc_fm[0:1, 42:43])
            gSE.wait(SE, gSE.n)
            SE.activation(out=rowb[0:1, ccol:ccol + npair],
                          in_=rowb[0:1, ccol:ccol + npair], func=AF.Exp, scale=0.5)
            gSE.wait(VE)
            VE.tensor_scalar(out=rowb[0:1, ccol:ccol + npair],
                             in0=rowb[0:1, ccol:ccol + npair],
                             scalar1=-40.0, scalar2=0.0, op0=OP.add, op1=OP.max)
            gEV.wait(TE)
            bfree(6, TE)
            for pair in pairs_q:
                TE.matmul(out=psY[:, pair:pair + 1], lhsT=on_f[0:1, :],
                          rhs=rowb[0:1, ccol + pair:ccol + pair + 1],
                          start=(pair == pairs_q[0]), stop=(pair == pairs_q[-1]))
            gPE.inc(TE.last)
            gPE.wait(VE)
            VE.tensor_scalar(out=negC[:, 0:len(pairs_q)], in0=psY[:, 0:len(pairs_q)],
                             scalar1=-1.0, scalar2=None, op0=OP.mult, op1=OP.bypass)
            bank_free[6] = (gEV, gEV.n)

        def attn_unit(pair, h_kv, qb, qh3, kh3, vh_fn, ctx_dst, causal, negC_col,
                      first_exp_waits=(), rscale=None, prev_finish=None, r_bf=None,
                      par=0, prev_tail=None):
            n_kt = (qb * 4 + 4) if causal else KT
            ngrp = n_kt // 4
            exp_marks = []
            if par == 0:
                psY_u, ybank = psY[:, 0:512], 6
                psM_u, mbank = psM, 7
            else:
                psY_u, ybank = psA[:, 0:512], 0
                psM_u, mbank = psA[:, 512:1024], 1

            def emit_scores(g4):
                for j in range(4):
                    kt = g4 * 4 + j
                    bfree(2 + j, TE)
                    inst = TE.matmul(out=psS[:, j * 512:(j + 1) * 512],
                                     lhsT=kh3[:, h_kv, kt * 128:(kt + 1) * 128],
                                     rhs=qh3[:, pair, qb * 512:(qb + 1) * 512],
                                     start=True, stop=True)
                    gPE.inc(inst)
                pev = gPE.n
                if g4 == 0:
                    for sm, v in first_exp_waits:
                        sm.wait(SE, v)
                if causal and g4 == qb:
                    gPE.wait(VE, pev)
                    for j in range(4):
                        VE.tensor_tensor(out=psS[:, j * 512:(j + 1) * 512],
                                         in0=psS[:, j * 512:(j + 1) * 512],
                                         in1=mask_sb[:, j, :], op=OP.add)
                    gEV.wait(SE)
                else:
                    gPE.wait(SE, pev)
                SE.activation(out=Dp[:, g4 * 4:g4 * 4 + 4, :].rearrange("p a b -> p (a b)"),
                              in_=psS[:, 0:2048], func=AF.Exp,
                              bias=negC[:, negC_col:negC_col + 1], scale=1.0)
                exp_marks.append(gSE.n)
                for j in range(4):
                    bank_free[2 + j] = (gSE, gSE.n)

            pv_last = [None]

            def emit_pv_quad(g4):
                gSE.wait(TE, exp_marks[g4])
                if g4 == 0:
                    bfree(ybank, TE)
                    bfree(mbank, TE)
                for j in range(4):
                    kt = g4 * 4 + j
                    TE.matmul(out=psY_u, lhsT=vh_fn(kt), rhs=Dp[:, kt, :],
                              start=(kt == 0), stop=(kt == n_kt - 1))
                for j in range(4):
                    kt = g4 * 4 + j
                    inst = TE.matmul(out=psM_u[0:1, 0:512], lhsT=on_b[:, 0:1],
                                     rhs=Dp[:, kt, :],
                                     start=(kt == 0), stop=(kt == n_kt - 1))
                pv_last[0] = inst

            if ngrp == 1 and prev_tail is not None:
                prev_tail()
            emit_scores(0)
            if ngrp > 1 and prev_tail is not None:
                prev_tail()
            if ngrp > 1:
                emit_scores(1)
            if prev_finish is not None:
                prev_finish()
            if ngrp > 1:
                emit_pv_quad(0)
                for g in range(2, ngrp):
                    emit_scores(g)
                    emit_pv_quad(g - 1)
            else:
                pass

            def tail():
                emit_pv_quad(ngrp - 1)
                gPE.inc(pv_last[0])
                pv_done = gPE.n
                gPE.wait(SE, pv_done)
                SE.activation(out=rowb[64:65, 512 * par:512 * par + 512],
                              in_=psM_u[0:1, 0:512], func=AF.Ln,
                              bias=sc_fm[0:1, 43:44])
                bank_free[mbank] = (gSE, gSE.n)
                gSE.wait(SE, gSE.n)
                SE.activation(out=rowb[64:65, 512 * par:512 * par + 512],
                              in_=rowb[64:65, 512 * par:512 * par + 512], func=AF.Exp,
                              scale=-1.0,
                              bias=(0.0 if rscale is None
                                    else sc_fm[64:65, rscale + 6:rscale + 7]))
                gSE.wait(VE)
                VE.tensor_copy(out=r_bf, in_=rowb[64:65, 512 * par:512 * par + 512])
                rec_done[0] = gEV.n

            rec_done = [0]

            def finish():
                gEV.wait(TE, rec_done[0])
                bfree(mbank, TE)
                TE.matmul(out=psM_u[:, 0:512], lhsT=on_b[0:1, :], rhs=r_bf,
                          start=True, stop=True)
                gPE.inc(TE.last)
                bc_done = gPE.n
                gPE.wait(VE, bc_done)
                VE.tensor_copy(out=rbc_sb, in_=psM_u[:, 0:512])
                bank_free[mbank] = (gEV, gEV.n)
                VE.tensor_tensor(out=ctx_dst, in0=psY_u, in1=rbc_sb, op=OP.mult)
                bank_free[ybank] = (gEV, gEV.n)
            return tail, finish

        def emit_ln(src, dst, eps, affine):
            gSE.wait(VE)
            gPE.wait(VE)
            dSf.wait(VE)
            dSb.wait(VE)
            dSn.wait(VE)
            cmarks, smarks = [], []
            for c in range(CH):
                VE.tensor_copy(out=Dn[:, c, :], in_=src[:, c, :])
                cmarks.append(gEV.n)
                gEV.wait(SE, cmarks[-1])
                SE.activation(out=Dsq[:, c, :], in_=Dn[:, c, :], func=AF.Square)
                smarks.append(gSE.n)
            cst = cmarks[-1]
            sqv = smarks[-1]
            bfree(7, TE)
            bfree(6, TE)
            for c in range(CH):
                gEV.wait(TE, cmarks[c])
                gSE.wait(TE, smarks[c])
                i1 = TE.matmul(out=psM[0:1, 0:512], lhsT=on_b[:, 0:1], rhs=Dn[:, c, :],
                               start=(c == 0), stop=(c == CH - 1))
                i2 = TE.matmul(out=psY[0:1, 0:512], lhsT=on_b[:, 0:1], rhs=Dsq[:, c, :],
                               start=(c == 0), stop=(c == CH - 1))
            gPE.inc(i2)
            st_done = gPE.n
            gPE.wait(VE, st_done)
            VE.tensor_scalar(out=rowb[0:1, 0:512], in0=psM[0:1, 0:512],
                             scalar1=1.0 / 1024, scalar2=None, op0=OP.mult, op1=OP.bypass)
            VE.tensor_scalar(out=rowb[0:1, 512:1024], in0=psY[0:1, 0:512],
                             scalar1=1.0 / 1024, scalar2=None, op0=OP.mult, op1=OP.bypass)
            VE.tensor_tensor(out=x1wf[0:1, 2048:2560], in0=rowb[0:1, 0:512],
                             in1=rowb[0:1, 0:512], op=OP.mult)
            VE.tensor_tensor(out=rowb[0:1, 512:1024], in0=rowb[0:1, 512:1024],
                             in1=x1wf[0:1, 2048:2560], op=OP.subtract)
            VE.tensor_scalar(out=rowb[0:1, 512:1024], in0=rowb[0:1, 512:1024],
                             scalar1=eps, scalar2=None, op0=OP.add, op1=OP.bypass)
            gEV.wait(SE)
            SE.activation(out=rowb[0:1, 512:1024], in_=rowb[0:1, 512:1024], func=AF.Ln)
            gSE.wait(SE, gSE.n)
            SE.activation(out=rowb[0:1, 512:1024], in_=rowb[0:1, 512:1024], func=AF.Exp,
                          scale=-0.5)
            fin = gSE.n
            gSE.wait(TE, fin)
            TE.matmul(out=psM[:, 0:512], lhsT=on_f[0:1, :], rhs=rowb[0:1, 0:512],
                      start=True, stop=True)
            TE.matmul(out=psY[:, 0:512], lhsT=on_f[0:1, :], rhs=rowb[0:1, 512:1024],
                      start=True, stop=True)
            gPE.inc(TE.last)
            bcv = gPE.n
            gPE.wait(VE, bcv)
            for c in range(CH):
                VE.tensor_tensor(out=src[:, c, :], in0=src[:, c, :], in1=psM[:, 0:512],
                                 op=OP.subtract)
            for c in range(CH):
                VE.tensor_tensor(out=dst[:, c, :], in0=src[:, c, :], in1=psY[:, 0:512],
                                 op=OP.mult)
                if affine:
                    VE.tensor_scalar(out=dst[:, c, :], in0=dst[:, c, :],
                                     scalar1=gb_fm[:, c, 0:1], scalar2=gb_fm[:, c, 1:2],
                                     op0=OP.mult, op1=OP.add)
                bank_free[7] = (gEV, gEV.n)
            bank_free[6] = (gEV, gEV.n)
            return cst

        stage_free = [None, None]
        for tt in range(KT):
            sl = tt % 2
            stg = x1wf[:, sl * 1024:(sl + 1) * 1024]
            if stage_free[sl] is not None:
                s, v = stage_free[sl]
                s.wait(SE, v)
            SE.dma_start(out=stg, in_=x_in[tt * 128:(tt + 1) * 128, :]).then_inc(dX[sl].h, 16)
            dX[sl].n += 16
            if tt == 0:
                bfree(2, TE)
            for _ in range(6):
                TE.matmul(out=psS[:, 0:512], lhsT=on_b[:], rhs=mask_sb[:, 0, :],
                          start=True, stop=True)
            dX[sl].wait(TE)
            for half, pbank, key in ((0, psM, 7), (1, psY, 6)):
                bfree(key, TE)
                for j in range(4):
                    c = half * 4 + j
                    inst = TE.transpose(out=pbank[:, j * 128:(j + 1) * 128],
                                        in_=stg[:, c * 128:(c + 1) * 128],
                                        identity=id_f[:])
                gPE.inc(inst)
                gPE.wait(VE)
                VE.tensor_copy(out=Dm3[:, half * 4:half * 4 + 4, tt * 128:(tt + 1) * 128],
                               in_=pbank[:, 0:512].rearrange("p (a b) -> p a b", a=4))
                bank_free[key] = (gEV, gEV.n)
            stage_free[sl] = (gPE, gPE.n)

        if upto == "load":
            return dump_exit(Dm3) or nc
        if upto == "w0":
            use_slot(0)
            gEV.wait(GP)
            GP.dma_start(out=fmv(dbg_t)[:, :, 0:1024],
                         in_=Wb0.ap().rearrange("p c n -> p c n")).then_inc(dO.h, 16)
            dO.n += 16
            dO.wait(GP, dO.n)
            finish_all()
            return nc
        gEV.wait(TE)
        proj_pass(0, lambda kc, tb, ns: Dm3[:, kc, tb * 1024 + ns * 512:tb * 1024 + ns * 512 + 512],
                  evac_plain_act(A3, scale=RS))

        if upto == "qh":
            return dump_exit(A3) or nc
        use_slot(1)
        Wkv = Wb[1]
        cv1 = conv_done[1]
        items3 = [(h, kv, tb, ns) for tb in range(NTB) for kv in range(2)
                  for h in range(2) for ns in range(2)]

        def mm_kv(gi, ps):
            h, kv, tb, ns = items3[gi]
            cv1[0].wait(TE, cv1[1])
            col = kv * 64 + h * 32
            for kc in range(CH):
                inst = TE.matmul(out=ps[0:32, :], lhsT=Wkv[:, kc, col:col + 32],
                                 rhs=Dm3[:, kc, tb * 1024 + ns * 512:tb * 1024 + ns * 512 + 512],
                                 start=(kc == 0), stop=(kc == CH - 1))
            return inst

        def ev_kv(gi, ps, pev):
            h, kv, tb, ns = items3[gi]
            dst = klat if kv == 0 else vlat
            gPE.wait(VE, pev)
            VE.tensor_copy(out=dst[:, h, tb * 1024 + ns * 512:tb * 1024 + ns * 512 + 512],
                           in_=ps[0:32, :])
            return (gEV, gEV.n)
        proj_groups(len(items3), mm_kv, ev_kv)
        pass_pe_end[1] = gPE.n

        if upto == "lat":
            return dump_exit(B3) or nc
        gEV.wait(TE, wkup_ready)
        gSE.wait(TE)
        for h in range(2):
            for ns in range(4):
                ps, bank = proj_bank()
                bfree(bank, TE)
                inst = TE.matmul(out=ps, lhsT=wkup[:, 0, :],
                                 rhs=klat[:, h, ns * 512:(ns + 1) * 512],
                                 start=True, stop=True)
                gPE.inc(inst)
                gPE.wait(VE)
                VE.tensor_copy(out=khA[:, h, ns * 512:(ns + 1) * 512], in_=ps)
                bank_free[bank] = (gEV, gEV.n)
                proj_g[0] += 1
        for h in range(2):
            for q4 in range(4):
                pbank, key = (psM, 7) if q4 % 2 == 0 else (psY, 6)
                bfree(key, TE)
                for j in range(4):
                    kt = q4 * 4 + j
                    inst = TE.matmul(out=pbank[:, j * 128:(j + 1) * 128],
                                     lhsT=vlat[:, h, kt * 128:(kt + 1) * 128],
                                     rhs=wkup[:, 1, :], start=(j == 0), stop=(j == 3))
                gPE.inc(inst)
                gPE.wait(VE)
                VE.tensor_copy(out=Cv[:, q4 * 4:q4 * 4 + 4, h * 128:h * 128 + 128],
                               in_=pbank[:, 0:512].rearrange("p (a b) -> p a b", a=4))
                bank_free[key] = (gEV, gEV.n)

        if upto == "kv":
            return dump_exit(B3) or nc
        if upto == "vh":
            return dump_exit(Cv.ap().rearrange("p a b -> p (a b)").rearrange("p (c t) -> p c t", c=CH)) or nc
        attn_norms(range(8), range(2), A3, khA, lambda p: p >> 2)
        if upto == "norm":
            gEV.wait(GP)
            GP.dma_start(out=dbg_t[0:128, 0:8], in_=negC[:]).then_inc(dO.h, 16)
            dO.n += 16
            dO.wait(GP, dO.n)
            finish_all()
            return nc
        use_slot(2)
        Wo_A = Wb[0]
        cvo = conv_done[2]
        gSE.wait(TE)
        gEV.wait(TE)
        x1_written = 0
        nstage_dma = 0
        ln_cst = [0]
        for qb in range(NQB):
            few = [(gEV, ln_cst[0])] if qb > 0 else []
            fins = []
            tails = []
            for pair in range(8):
                kvh = pair >> 2
                pf = fins[-2] if len(fins) >= 2 else None
                pt = tails[-1] if tails else None
                t, f = attn_unit(pair, kvh, qb, A3, khA,
                                 lambda kt, kvh=kvh: Cv[:, kt, kvh * 128:kvh * 128 + 128],
                                 Dc[:, pair, :], True, pair,
                                 first_exp_waits=(few if pair == 0 else []),
                                 prev_finish=pf, r_bf=Dq[0:1, pair % 2, 0:512],
                                 par=pair % 2, prev_tail=pt)
                tails.append(t)
                fins.append(f)
            tails[-1]()
            fins[-2]()
            fins[-1]()
            for tt in range(4):
                sl = tt % 2
                stg = x1wf[:, sl * 1024:(sl + 1) * 1024]
                if stage_free[sl] is not None:
                    s, v = stage_free[sl]
                    s.wait(SE, v)
                gEV.wait(SE)
                dSb.wait(SE)
                dSf.wait(SE)
                SE.dma_start(out=stg, in_=x_in[qb * 512 + tt * 128: qb * 512 + (tt + 1) * 128, :]
                             ).then_inc(dX[sl].h, 16)
                dX[sl].n += 16
                dX[sl].wait(TE)
                for half, pbank, key in ((0, psM, 7), (1, psY, 6)):
                    bfree(key, TE)
                    for j in range(4):
                        c = half * 4 + j
                        inst = TE.transpose(out=pbank[:, j * 128:(j + 1) * 128],
                                            in_=stg[:, c * 128:(c + 1) * 128],
                                            identity=id_f[:])
                    gPE.inc(inst)
                    gPE.wait(VE)
                    VE.tensor_copy(out=LNw[:, half * 4:half * 4 + 4, tt * 128:(tt + 1) * 128],
                                   in_=pbank[:, 0:512].rearrange("p (a b) -> p a b", a=4))
                    bank_free[key] = (gEV, gEV.n)
                stage_free[sl] = (gPE, gPE.n)

            gEV.wait(TE)

            def mm_o(gi, ps):
                cvo[0].wait(TE, cvo[1])
                for kc in range(CH):
                    inst = TE.matmul(out=ps, lhsT=Wo_A[:, kc, gi * 128:(gi + 1) * 128],
                                     rhs=Dc[:, kc, :], start=(kc == 0), stop=(kc == CH - 1))
                return inst

            def ev_o(gi, ps, pev):
                gPE.wait(VE, pev)
                VE.tensor_tensor(out=LNw[:, gi, :], in0=LNw[:, gi, :], in1=ps, op=OP.add)
                return (gEV, gEV.n)
            proj_groups(CH, mm_o, ev_o)
            ln_cst[0] = emit_ln(LNw, x1w, EPS_A, False)
            wb = gEV.n
            gEV.wait(GP, wb)
            GP.dma_start(out=fmv(x1b_t)[:, :, qb * 512:(qb + 1) * 512],
                         in_=x1w[:, :, :]).then_inc(dSb.h, 16)
            dSb.n += 16
            gEV.wait(SY, wb)
            SY.dma_start(out=fmv(x1f_t)[:, :, qb * 512:(qb + 1) * 512],
                         in_=x1w[:, :, :]).then_inc(dSf.h, 16)
            dSf.n += 16
        pass_pe_end[2] = gPE.n
        a_free[0] = gPE.n

        if upto == "A":
            if dbg_t is not None:
                dSb.wait(SY)
                SY.dma_start(out=dbg_t, in_=x1b_t).then_inc(dO.h, 16)
                dO.n += 16
            finish_all()
            return nc

        node_write_dma = {-1: (dSb, dSb.n)}

        def edge_pass_full(wslot, src_dram, eidx, src_ready, first_waits=(),
                           prev_read=None):
            evac = evac_bias_dve(Dm3, eidx, first_waits=first_waits)
            pr = a_free[0] if prev_read is None else prev_read
            for tb in range(NTB):
                src_ready[0].wait(SE, src_ready[1])
                gPE.wait(SE, pr)
                SE.dma_start(out=A3[:, :, (tb % 2) * 1024:(tb % 2) * 1024 + 512],
                             in_=fmv(src_dram)[:, :, tb * 1024:tb * 1024 + 512]
                             ).then_inc(dEa[tb % 2].h, 16)
                dEa[tb % 2].n += 16
                SE.dma_start(out=A3[:, :, (tb % 2) * 1024 + 512:(tb % 2) * 1024 + 1024],
                             in_=fmv(src_dram)[:, :, tb * 1024 + 512:(tb + 1) * 1024]
                             ).then_inc(dEb[tb % 2].h, 16)
                dEb[tb % 2].n += 16
                proj_pass(wslot,
                          lambda kc, tb_, ns: A3[:, kc, (tb % 2) * 1024 + ns * 512:(tb % 2) * 1024 + ns * 512 + 512],
                          evac, tbs=[tb],
                          ns_waits=[(dEa[tb % 2], dEa[tb % 2].n),
                                    (dEb[tb % 2], dEb[tb % 2].n)])

        for c, (eq, inq, ek, ink, ev, inv) in enumerate(NODES):
            ords = list(slot_order[c])
            share_all = (c == 0)
            share_qv = (eq == ev and inq == inv and not share_all)
            si = 0
            e_first = eq if (share_all or share_qv) else ev
            in_first = inq if (share_all or share_qv) else inv
            gEV.wait(GP)
            edge_pass_full(ords[si], node_src[in_first], e_first,
                           node_write_dma[in_first],
                           first_waits=[(dSn, dSn.n), (dSq, dSq.n), (dSb, dSb.n)])
            si += 1
            use_slot(ords[si])
            Wv_s = Wb[ords[si] % 2]
            cvv = conv_done[ords[si]]
            gEV.wait(TE)
            gSE.wait(TE)
            for kt in range(KT):
                for ns in range(2):
                    ps, bank = proj_bank()
                    bfree(bank, TE)
                    cvv[0].wait(TE, cvv[1])
                    for kc in range(CH):
                        inst = TE.matmul(out=ps, lhsT=Dm3[:, kc, kt * 128:(kt + 1) * 128],
                                         rhs=Wv_s[:, kc, ns * 512:(ns + 1) * 512],
                                         start=(kc == 0), stop=(kc == CH - 1))
                    gPE.inc(inst)
                    gPE.wait(SE)
                    SE.copy(out=Cv[:, kt, ns * 512:(ns + 1) * 512], in_=ps)
                    bank_free[bank] = (gSE, gSE.n)
                    proj_g[0] += 1
            pass_pe_end[ords[si]] = gPE.n
            si += 1
            if not (share_all or share_qv):
                gSE.wait(GP)
                edge_pass_full(ords[si], node_src[inq], eq, node_write_dma[inq],
                               first_waits=[(dSq, dSq.n)],
                               prev_read=pass_pe_end[ords[0]])
                si += 1
            gEV.wait(TE)
            proj_pass(ords[si],
                      lambda kc, tb, ns: Dm3[:, kc, tb * 1024 + ns * 512:tb * 1024 + ns * 512 + 512],
                      evac_plain_act(A3, scale=sc_fm[:, 24 + c:25 + c])); si += 1
            gEV.wait(SY)
            SY.dma_start(out=fmv(qsp_t), in_=Dm3[:, :, :]).then_inc(dSq.h, 16)
            dSq.n += 16
            qsp_dma = dSq.n
            if share_all:
                proj_pass(ords[si],
                          lambda kc, tb, ns: Dm3[:, kc, tb * 1024 + ns * 512:tb * 1024 + ns * 512 + 512],
                          evac_plain_act(B3, scale=sc_fm[:, 3 * c + 1:3 * c + 2])); si += 1
            else:
                ek_slot, kh_slot = ords[si], ords[si + 1]
                si += 2
                for tb in range(NTB):
                    node_write_dma[ink][0].wait(SE, node_write_dma[ink][1])
                    gPE.wait(SE, a_free[0])
                    SE.dma_start(out=B3[:, :, (tb % 2) * 1024:(tb % 2) * 1024 + 512],
                                 in_=fmv(node_src[ink])[:, :, tb * 1024:tb * 1024 + 512]
                                 ).then_inc(dNk[tb % 2].h, 16)
                    dNk[tb % 2].n += 16
                    SE.dma_start(out=B3[:, :, (tb % 2) * 1024 + 512:(tb % 2) * 1024 + 1024],
                                 in_=fmv(node_src[ink])[:, :, tb * 1024 + 512:(tb + 1) * 1024]
                                 ).then_inc(dNkb[tb % 2].h, 16)
                    dNkb[tb % 2].n += 16
                    first_ev = [True]

                    def evac_ek(m, tb_, ns, ps, pev):
                        gPE.wait(VE, pev)
                        if first_ev[0]:
                            dSq.wait(VE, qsp_dma)
                            first_ev[0] = False
                        col = tb_ * 1024 + ns * 512
                        VE.tensor_scalar(out=Dm3[:, m, col:col + 512], in0=ps,
                                         scalar1=be_fm[:, m, ek:ek + 1], scalar2=None,
                                         op0=OP.add, op1=OP.bypass)
                        return (gEV, gEV.n)
                    proj_pass(ek_slot,
                              lambda kc, tb_, ns: B3[:, kc, (tb_ % 2) * 1024 + ns * 512:(tb_ % 2) * 1024 + ns * 512 + 512],
                              evac_ek, tbs=[tb],
                              ns_waits=[(dNk[tb % 2], dNk[tb % 2].n),
                                        (dNkb[tb % 2], dNkb[tb % 2].n)])
                gEV.wait(TE)
                proj_pass(kh_slot,
                          lambda kc, tb, ns: Dm3[:, kc, tb * 1024 + ns * 512:tb * 1024 + ns * 512 + 512],
                          evac_plain_act(B3, scale=sc_fm[:, 3 * c + 1:3 * c + 2]))
            attn_norms(range(8), range(8), A3, B3, lambda p: p,
                       extra_waits=[(dSq, qsp_dma)])
            wo_slot = ords[si]; si += 1
            use_slot(wo_slot)
            Wo_s = Wb[wo_slot % 2]
            cvo2 = conv_done[wo_slot]
            is_last = (c == 5)
            attn_pe_end = [0]
            gSE.wait(TE)
            gEV.wait(TE)
            for qb in range(NQB):
                dSq.wait(SE, qsp_dma)
                gEV.wait(SE)
                gPE.wait(SE)
                SE.dma_start(out=Dq, in_=fmv(qsp_t)[:, :, qb * 512:(qb + 1) * 512]
                             ).then_inc(dQr.h, 16)
                dQr.n += 16
                qrl = dQr.n
                few = [(gEV, gEV.n)] if (is_last and qb > 0) else []
                fins = []
                tails = []
                for h in range(8):
                    pf = fins[-2] if len(fins) >= 2 else None
                    pt = tails[-1] if tails else None
                    t, f = attn_unit(h, h, qb, A3, B3,
                                     lambda kt, h=h: Cv[:, kt, h * 128:(h + 1) * 128],
                                     Dc[:, h, :], False, h,
                                     first_exp_waits=(few if h == 0 else []),
                                     rscale=30 + c, prev_finish=pf,
                                     r_bf=mask_sb[0:1, h % 2, :], par=h % 2,
                                     prev_tail=pt)
                    tails.append(t)
                    fins.append(f)
                tails[-1]()
                fins[-2]()
                fins[-1]()
                if qb == NQB - 1:
                    attn_pe_end[0] = gPE.n
                gEV.wait(TE)

                def mm_o2(gi, ps):
                    cvo2[0].wait(TE, cvo2[1])
                    for kc in range(CH):
                        inst = TE.matmul(out=ps, lhsT=Wo_s[:, kc, gi * 128:(gi + 1) * 128],
                                         rhs=Dc[:, kc, :], start=(kc == 0), stop=(kc == CH - 1))
                    return inst

                def ev_o2(gi, ps, pev):
                    gPE.wait(VE, pev)
                    dQr.wait(VE, qrl)
                    if is_last:
                        dO.wait(VE)
                    else:
                        dSn.wait(VE)
                        dSf.wait(VE)
                        dSb.wait(VE)
                    dst = LNw[:, gi, :] if is_last else x1w[:, gi, :]
                    VE.scalar_tensor_tensor(out=dst, in0=Dq[:, gi, :],
                                            scalar=sc_fm[:, 3 * c:3 * c + 1], in1=ps,
                                            op0=OP.mult, op1=OP.add)
                    return (gEV, gEV.n)
                proj_groups(CH, mm_o2, ev_o2)
                if not is_last:
                    nwb = gEV.n
                    gEV.wait(GP, nwb)
                    GP.dma_start(out=fmv(nsc_t[c])[:, :, qb * 512:(qb + 1) * 512],
                                 in_=x1w[:, :, :]).then_inc(dSn.h, 16)
                    dSn.n += 16
                else:
                    emit_ln(LNw, LNw, EPS_F, True)
                    dSf.wait(SE)
                    dSn.wait(SE)
                    gEV.wait(SE)
                    gPE.wait(SE)
                    SE.dma_start(out=x1w[:, :, :],
                                 in_=fmv(x1f_t)[:, :, qb * 512:(qb + 1) * 512]
                                 ).then_inc(dXf.h, 16)
                    dXf.n += 16
                    x1r = dXf.n
                    dXf.wait(VE, x1r)
                    for cc in range(CH):
                        VE.tensor_tensor(out=LNw[:, cc, :], in0=LNw[:, cc, :],
                                         in1=x1w[:, cc, :], op=OP.add)
                    emit_ln(LNw, x1w, EPS_A, False)
                    for tt in range(4):
                        for half, pbank, key in ((0, psM, 7), (1, psY, 6)):
                            bfree(key, TE)
                            gEV.wait(TE)
                            for j in range(4):
                                cc = half * 4 + j
                                inst = TE.transpose(out=pbank[:, j * 128:(j + 1) * 128],
                                                    in_=x1w[:, cc, tt * 128:(tt + 1) * 128],
                                                    identity=id_f[:])
                            gPE.inc(inst)
                            gPE.wait(VE)
                            dO.wait(VE)
                            VE.tensor_copy(
                                out=LNwf[:, half * 512:(half + 1) * 512].rearrange(
                                    "p (a b) -> p a b", a=4),
                                in_=pbank[:, 0:512].rearrange("p (a b) -> p a b", a=4))
                            bank_free[key] = (gEV, gEV.n)
                        wrt = gEV.n
                        gEV.wait(SY, wrt)
                        SY.dma_start(out=out_t[qb * 512 + tt * 128: qb * 512 + (tt + 1) * 128, :],
                                     in_=LNwf[:, 0:1024]).then_inc(dO.h, 16)
                        dO.n += 16
            pass_pe_end[wo_slot] = gPE.n
            a_free[0] = attn_pe_end[0]
            node_write_dma[c] = (dSn, dSn.n)
            if dbg_t is not None and dbg_node == c and c < 5:
                dSn.wait(SY)
                SY.dma_start(out=dbg_t, in_=nsc_t[c]).then_inc(dO.h, 16)
                dO.n += 16
            if upto == c:
                finish_all()
                return nc

        finish_all()
    return nc


_NC_CACHE = {}


def _get_nc(upto="full", dbg_node=None):
    key = (upto, dbg_node)
    if key not in _NC_CACHE:
        _NC_CACHE[key] = build_nc(upto, dbg_node)
    return _NC_CACHE[key]


def make_consts():
    bf = ml_dtypes.bfloat16
    ident = np.eye(128, dtype=np.float32)
    ones_f = np.ones((128, 128), dtype=np.float32)
    ones_b = np.ones((128, 128), dtype=bf)
    p = np.arange(128)[:, None, None]
    si = np.arange(4)[None, :, None]
    j = np.arange(512)[None, None, :]
    mask = np.where(j - si * 128 - p >= 0, 0.0, -1e30).astype(bf)
    return {"ident_f": ident, "ones_f": ones_f, "ones_b": ones_b, "mask": mask}


def make_inmaps(inputs):
    consts = make_consts()
    shared = {
        "attn_wq": inputs["L0_attn_wq"], "attn_wk_down": inputs["L0_attn_wk_down"],
        "attn_wv_down": inputs["L0_attn_wv_down"], "attn_wk_up": inputs["L0_attn_wk_up"],
        "attn_wv_up": inputs["L0_attn_wv_up"], "attn_wo": inputs["L0_attn_wo"],
        "ffn_ew": inputs["L0_ffn_ew"], "ffn_act_w": inputs["L0_ffn_act_w"],
        "ffn_We": inputs["L0_ffn_We"], "ffn_be": inputs["L0_ffn_be"],
        "ffn_Wq": inputs["L0_ffn_Wq"], "ffn_Wk": inputs["L0_ffn_Wk"],
        "ffn_Wv": inputs["L0_ffn_Wv"], "ffn_Wo": inputs["L0_ffn_Wo"],
        "ffn_ln_g": inputs["L0_ffn_ln_g"], "ffn_ln_b": inputs["L0_ffn_ln_b"],
    }
    shared = {k: np.ascontiguousarray(np.asarray(v), dtype=np.float32)
              for k, v in shared.items()}
    shared.update(consts)
    x = np.asarray(inputs["L0_attn_x"])
    maps = []
    for b in range(8):
        m = dict(shared)
        m["x"] = np.ascontiguousarray(x[b], dtype=np.float32)
        maps.append(m)
    return maps


def kernel(**inputs):
    nc = _get_nc("full", None)
    maps = make_inmaps(inputs)
    res = run_bass_kernel_spmd(nc, maps, core_ids=list(range(8)))
    return np.stack([np.asarray(res.results[i]["out"]) for i in range(8)], axis=0)


# version 20 (final)
# speedup vs baseline: 1.0145x; 1.0145x over previous
import contextlib
import numpy as np
import ml_dtypes
import concourse.bass as bass
import concourse.mybir as mybir
from concourse.bass_utils import run_bass_kernel_spmd

F32 = mybir.dt.float32
BF16 = mybir.dt.bfloat16
AF = mybir.ActivationFunctionType
OP = mybir.AluOpType

P = 128
CH = 8
T = 2048
NTB = 2
NQB = 4
KT = 16
RS = 0.08838834764831845
EPS_A = 1e-5
EPS_F = 1e-6

NODES = [
    (0, -1, 0, -1, 0, -1),
    (2, 0, 1, -1, 2, 0),
    (5, 1, 3, -1, 4, 0),
    (9, 2, 6, -1, 8, 1),
    (14, 3, 10, -1, 12, 1),
    (19, 4, 15, 0, 17, 2),
]


_COMPUTE_OPS = {
    "tensor_copy", "tensor_scalar", "tensor_tensor", "scalar_tensor_tensor",
    "tensor_reduce", "reduce_max", "reduce_sum", "reciprocal", "memset",
    "activation", "copy", "mul", "add", "sqrt", "square", "iota", "select",
    "tensor_scalar_mul", "tensor_scalar_add", "tensor_add", "tensor_sub",
    "tensor_mul", "tensor_max", "reciprocal_approx_accurate",
    "reciprocal_approx_fast",
}


class EngWrap:

    def __init__(self, eng, autosem=None):
        object.__setattr__(self, "_e", eng)
        object.__setattr__(self, "last", None)
        object.__setattr__(self, "auto", autosem)

    def __getattr__(self, n):
        f = getattr(self._e, n)
        if callable(f):
            def g(*a, **k):
                auto = self.auto
                if auto is not None and n in _COMPUTE_OPS and auto.n > 0:
                    self._e.wait_ge(auto.h, auto.n)
                r = f(*a, **k)
                if hasattr(r, "then_inc"):
                    object.__setattr__(self, "last", r)
                    if auto is not None and n in _COMPUTE_OPS:
                        r.then_inc(auto.h, 1)
                        auto.n += 1
                return r
            return g
        return f


class Sem:
    def __init__(self, nc, ctx, name):
        self.h = ctx.enter_context(nc.semaphore(name))
        self.n = 0

    def inc(self, inst, k=1):
        inst.then_inc(self.h, k)
        self.n += k
        return self.n

    def wait(self, eng, val=None):
        v = self.n if val is None else val
        assert v <= self.n, f"wait beyond produced count {v} > {self.n}"
        if v > 0:
            eng.wait_ge(self.h, v)


def build_nc(upto="full", dbg_node=None):
    nc = bass.Bass("TRN2", target_bir_lowering=False)

    x_in = nc.dram_tensor("x", [T, 1024], F32, kind="ExternalInput").ap()
    wq_a = nc.dram_tensor("attn_wq", [1024, 1024], F32, kind="ExternalInput").ap()
    wkd_a = nc.dram_tensor("attn_wk_down", [1024, 64], F32, kind="ExternalInput").ap()
    wvd_a = nc.dram_tensor("attn_wv_down", [1024, 64], F32, kind="ExternalInput").ap()
    wku_a = nc.dram_tensor("attn_wk_up", [32, 128], F32, kind="ExternalInput").ap()
    wvu_a = nc.dram_tensor("attn_wv_up", [32, 128], F32, kind="ExternalInput").ap()
    wo_a = nc.dram_tensor("attn_wo", [1024, 1024], F32, kind="ExternalInput").ap()
    ew_in = nc.dram_tensor("ffn_ew", [6, 3], F32, kind="ExternalInput").ap()
    aw_in = nc.dram_tensor("ffn_act_w", [6], F32, kind="ExternalInput").ap()
    We_in = nc.dram_tensor("ffn_We", [20, 1024, 1024], F32, kind="ExternalInput").ap()
    be_in = nc.dram_tensor("ffn_be", [20, 1024], F32, kind="ExternalInput").ap()
    Wq_in = nc.dram_tensor("ffn_Wq", [6, 1024, 1024], F32, kind="ExternalInput").ap()
    Wk_in = nc.dram_tensor("ffn_Wk", [6, 1024, 1024], F32, kind="ExternalInput").ap()
    Wv_in = nc.dram_tensor("ffn_Wv", [6, 1024, 1024], F32, kind="ExternalInput").ap()
    Wo_in = nc.dram_tensor("ffn_Wo", [6, 1024, 1024], F32, kind="ExternalInput").ap()
    lng_in = nc.dram_tensor("ffn_ln_g", [1024], F32, kind="ExternalInput").ap()
    lnb_in = nc.dram_tensor("ffn_ln_b", [1024], F32, kind="ExternalInput").ap()
    ident_f = nc.dram_tensor("ident_f", [128, 128], F32, kind="ExternalInput").ap()
    ones_f = nc.dram_tensor("ones_f", [128, 128], F32, kind="ExternalInput").ap()
    ones_b = nc.dram_tensor("ones_b", [128, 128], BF16, kind="ExternalInput").ap()
    mask_in = nc.dram_tensor("mask", [128, 4, 512], BF16, kind="ExternalInput").ap()

    out_t = nc.dram_tensor("out", [T, 1024], F32, kind="ExternalOutput").ap()
    dbg_t = None
    if dbg_node is not None:
        dbg_t = nc.dram_tensor("dbg", [1024, T], BF16, kind="ExternalOutput").ap()

    x1b_t = nc.dram_tensor("x1b", [1024, T], BF16, kind="Internal").ap()
    x1f_t = nc.dram_tensor("x1f", [1024, T], F32, kind="Internal").ap()
    qsp_t = nc.dram_tensor("qsp", [1024, T], BF16, kind="Internal").ap()
    nsc_t = [nc.dram_tensor(f"nsc{c}", [1024, T], BF16, kind="Internal").ap()
             for c in range(5)]

    def fmv(dram):
        return dram.rearrange("(c p) t -> p c t", p=128)

    node_src = {-1: x1b_t}
    for c in range(5):
        node_src[c] = nsc_t[c]

    ctx = contextlib.ExitStack()
    with ctx:
        E = ctx.enter_context
        A = E(nc.sbuf_tensor("A", [P, CH, T], BF16))
        B = E(nc.sbuf_tensor("B", [P, CH, T], BF16))
        Cv = E(nc.sbuf_tensor("Cv", [P, KT, 1024], BF16))
        Dm = E(nc.sbuf_tensor("Dm", [P, CH * T], BF16))
        Wb0 = E(nc.sbuf_tensor("Wb0", [P, CH, 1024], BF16))
        Wb1 = E(nc.sbuf_tensor("Wb1", [P, CH, 1024], BF16))
        Wb = [Wb0, Wb1]
        wst = E(nc.sbuf_tensor("wst", [P, 1024], F32))
        LNw = E(nc.sbuf_tensor("LNw", [P, CH, 512], F32))
        x1w = E(nc.sbuf_tensor("x1w", [P, CH, 512], F32))
        mask_sb = E(nc.sbuf_tensor("mask_sb", [P, 4, 512], BF16))
        id_f = E(nc.sbuf_tensor("id_f", [P, 128], F32))
        on_f = E(nc.sbuf_tensor("on_f", [P, 128], F32))
        on_b = E(nc.sbuf_tensor("on_b", [P, 128], BF16))
        be_fm = E(nc.sbuf_tensor("be_fm", [P, CH, 20], F32))
        gb_fm = E(nc.sbuf_tensor("gb_fm", [P, CH, 2], F32))
        sc_fm = E(nc.sbuf_tensor("sc_fm", [P, 44], F32))
        rowb = E(nc.sbuf_tensor("rowb", [P, 1024], F32))
        wkup = E(nc.sbuf_tensor("wkup", [32, 2, 128], BF16))
        negC = E(nc.sbuf_tensor("negC", [P, 8], F32))

        psA = E(nc.psum_tensor("psA", [P, 1024], F32))
        psS = E(nc.psum_tensor("psS", [P, 2048], F32))
        psY = E(nc.psum_tensor("psY", [P, 512], F32))
        psM = E(nc.psum_tensor("psM", [P, 512], F32))

        A3 = A.ap()
        B3 = B.ap()
        Dmf = Dm.ap()
        Dm3 = Dmf.rearrange("p (c t) -> p c t", c=CH)
        Dp = Dmf[:, 0:8192].rearrange("p (k q) -> p k q", k=16)
        Dc = Dmf[:, 8192:12288].rearrange("p (c q) -> p c q", c=CH)
        Dq = Dmf[:, 12288:16384].rearrange("p (c q) -> p c q", c=CH)
        Dn = Dmf[:, 0:4096].rearrange("p (c q) -> p c q", c=CH)
        Dsq = Dmf[:, 4096:8192].rearrange("p (c q) -> p c q", c=CH)
        x1wf = x1w.ap().rearrange("p c t -> p (c t)")
        LNwf = LNw.ap().rearrange("p c t -> p (c t)")
        rbc_sb = rowb.ap()[:, 512:1024]
        klat = B.ap()[0:32, 4:6, :]
        vlat = B.ap()[0:32, 6:8, :]
        khA = B.ap()[:, 0:2, :]

        dW = Sem(nc, ctx, "dW")
        dA = Sem(nc, ctx, "dA")
        dO = Sem(nc, ctx, "dO")
        dX = [Sem(nc, ctx, "dX0"), Sem(nc, ctx, "dX1")]
        dEa = [Sem(nc, ctx, "dEa0"), Sem(nc, ctx, "dEa1")]
        dEb = [Sem(nc, ctx, "dEb0"), Sem(nc, ctx, "dEb1")]
        dNkb = [Sem(nc, ctx, "dNkb0"), Sem(nc, ctx, "dNkb1")]
        dNk = [Sem(nc, ctx, "dNk0"), Sem(nc, ctx, "dNk1")]
        dQr = Sem(nc, ctx, "dQr")
        dXf = Sem(nc, ctx, "dXf")
        dSq = Sem(nc, ctx, "dSq")
        dSb = Sem(nc, ctx, "dSb")
        dSf = Sem(nc, ctx, "dSf")
        dSn = Sem(nc, ctx, "dSn")
        gPE = Sem(nc, ctx, "gPE")
        gEV = Sem(nc, ctx, "gEV")
        gSE = Sem(nc, ctx, "gSE")

        bank_free = {}
        a_free = [0]

        def bfree(key, eng):
            if key in bank_free:
                s, v = bank_free[key]
                s.wait(eng, v)

        TE, VE, SE, SY, GP = (EngWrap(nc.tensor), EngWrap(nc.vector, gEV),
                              EngWrap(nc.scalar, gSE), EngWrap(nc.sync), EngWrap(nc.gpsimd))

        def finish_all(eng=None):
            for sem, e in ((dW, SY), (dSq, SY), (dSb, SY), (dSf, SY), (dSn, SY),
                           (dA, GP), (dQr, GP), (dXf, GP), (dNk[0], GP), (dNk[1], GP),
                           (dNkb[0], GP), (dNkb[1], GP), (dEb[0], GP), (dEb[1], GP),
                           (dX[0], GP), (dX[1], GP), (dEa[0], GP), (dEa[1], GP),
                           (dO, SY)):
                if sem.n > 0:
                    sem.wait(e, sem.n)

        def dump_exit(src3, ncols=T):
            gPE.wait(SY); gEV.wait(SY); gSE.wait(SY); dW.wait(SY); 
            SY.dma_start(out=fmv(dbg_t)[:, :, 0:ncols], in_=src3).then_inc(dO.h, 16)
            dO.n += 16
            finish_all()

        for dst, src in ((id_f[:], ident_f), (on_f[:], ones_f), (on_b[:], ones_b),
                         (mask_sb[:], mask_in), (wst[0:20, :], be_in),
                         (rowb[32:33, 0:1024], lng_in[None, :]),
                         (rowb[64:65, 0:1024], lnb_in[None, :]),
                         (rowb[0:1, 0:18], ew_in.rearrange("a b -> (a b)")[None, :]),
                         (rowb[0:1, 18:24], aw_in[None, :])):
            SY.dma_start(out=dst, in_=src).then_inc(dW.h, 16)
            dW.n += 16
        init_dma = dW.n

        if upto == "init0":
            dW.wait(GP, init_dma)
            GP.dma_start(out=dbg_t[0:128, 0:128], in_=id_f[:]).then_inc(dO.h, 16)
            dO.n += 16
            dO.wait(GP, dO.n)
            finish_all()
            return nc
        dW.wait(TE, init_dma)
        for c in range(CH):
            dst = psM[:, c * 20:(c + 1) * 20] if c < 4 else psY[:, (c - 4) * 20:(c - 3) * 20]
            TE.transpose(out=dst, in_=wst[0:20, c * 128:(c + 1) * 128], identity=id_f[0:20, 0:20])
        for c in range(CH):
            TE.transpose(out=psS[:, c:c + 1], in_=rowb[32:33, c * 128:(c + 1) * 128],
                         identity=on_f[32:33, 0:1])
            TE.transpose(out=psS[:, 512 + c:513 + c], in_=rowb[64:65, c * 128:(c + 1) * 128],
                         identity=on_f[64:65, 0:1])
        TE.matmul(out=psS[:, 1024:1048], lhsT=on_f[0:1, :], rhs=rowb[0:1, 0:24],
                  start=True, stop=True)
        gPE.inc(TE.last)

        gPE.wait(VE)
        for c in range(CH):
            src = psM[:, c * 20:(c + 1) * 20] if c < 4 else psY[:, (c - 4) * 20:(c - 3) * 20]
            VE.tensor_copy(out=be_fm[:, c, :], in_=src)
            VE.tensor_copy(out=gb_fm[:, c, 0:1], in_=psS[:, c:c + 1])
            VE.tensor_copy(out=gb_fm[:, c, 1:2], in_=psS[:, 512 + c:513 + c])
        VE.tensor_copy(out=sc_fm[:, 0:24], in_=psS[:, 1024:1048])
        for c in range(6):
            VE.tensor_scalar(out=sc_fm[:, 24 + c:25 + c], in0=sc_fm[:, 3 * c:3 * c + 1],
                             scalar1=RS, scalar2=None, op0=OP.mult, op1=OP.bypass)
            VE.tensor_tensor(out=sc_fm[:, 30 + c:31 + c], in0=sc_fm[:, 3 * c + 2:3 * c + 3],
                             in1=sc_fm[:, 18 + c:19 + c], op=OP.mult)
        VE.memset(sc_fm[:, 42:43], 1e-30)
        VE.memset(sc_fm[:, 43:44], 1e-19)
        gEV.wait(SE)
        SE.activation(out=sc_fm[:, 36:42], in_=sc_fm[:, 30:36], func=AF.Ln,
                      bias=sc_fm[:, 42:43])
        for key in range(8):
            bank_free[key] = (gEV, gEV.n)

        gPE.wait(SY)
        SY.dma_start(out=wst[0:32, 0:128], in_=wku_a).then_inc(dW.h, 16); dW.n += 16
        SY.dma_start(out=wst[0:32, 128:256], in_=wvu_a).then_inc(dW.h, 16); dW.n += 16
        dW.wait(VE)
        VE.tensor_copy(out=wkup[:, 0, :], in_=wst[0:32, 0:128])
        VE.tensor_copy(out=wkup[:, 1, :], in_=wst[0:32, 128:256])
        wkup_ready = gEV.n
        wst_user = [(gEV, gEV.n)]

        if upto == "init":
            gEV.wait(GP)
            GP.dma_start(out=dbg_t[0:128, 0:160],
                         in_=be_fm.ap().rearrange("p c e -> p (c e)")).then_inc(dO.h, 16)
            dO.n += 16
            dO.wait(GP, dO.n)
            finish_all()
            return nc
        dWw = [Sem(nc, ctx, "dWw0"), Sem(nc, ctx, "dWw1")]
        wslots = []

        def add_slot(ap2d):
            wslots.append([(ap2d.rearrange("(c p) n -> p c n", p=128), 0, ap2d.shape[1])])

        def add_slot_pair(apa, apb):
            wslots.append([(apa.rearrange("(c p) n -> p c n", p=128), 0, 64),
                           (apb.rearrange("(c p) n -> p c n", p=128), 64, 64)])

        add_slot(wq_a)
        add_slot_pair(wkd_a, wvd_a)
        add_slot(wo_a)
        slot_order = []
        slot_scale = {}
        for c, (eq, inq, ek, ink, ev, inv) in enumerate(NODES):
            if c == 0:
                specs = [We_in[eq], Wv_in[c], Wq_in[c], Wk_in[c], Wo_in[c]]
            elif eq == ev and inq == inv:
                specs = [We_in[eq], Wv_in[c], Wq_in[c], We_in[ek], Wk_in[c], Wo_in[c]]
            else:
                specs = [We_in[ev], Wv_in[c], We_in[eq], Wq_in[c], We_in[ek],
                         Wk_in[c], Wo_in[c]]
            ords = []
            for ap2d in specs:
                ords.append(len(wslots))
                add_slot(ap2d)
            slot_order.append(ords)

        NSLOT = len(wslots)
        conv_done = [None] * NSLOT
        pass_pe_end = {}
        next_load = [0]

        def emit_wload(k):
            dst = Wb[k % 2]
            dep = pass_pe_end.get(k - 2, 0)
            if dep:
                gPE.wait(GP, dep)
            for v3, col0, ncol in wslots[k]:
                out = dst[:, :, col0:col0 + ncol] if ncol < 1024 else dst[:, :, :]
                GP.dma_start(out=out, in_=v3).then_inc(dWw[k % 2].h, 16)
                dWw[k % 2].n += 16
            conv_done[k] = (dWw[k % 2], dWw[k % 2].n)

        def use_slot(k):
            while next_load[0] <= min(k + 1, NSLOT - 1):
                emit_wload(next_load[0])
                next_load[0] += 1
            assert conv_done[k] is not None

        proj_g = [0]

        def proj_bank():
            bank = proj_g[0] % 4
            ps = (psA[:, bank * 512:(bank + 1) * 512] if bank < 2
                  else psS[:, (bank - 2) * 512:(bank - 1) * 512])
            return ps, bank

        def proj_groups(n_groups, mm_emit, evac_emit):
            for gi in range(n_groups):
                ps, bank = proj_bank()
                bfree(bank, TE)
                last = mm_emit(gi, ps)
                gPE.inc(last)
                pev = gPE.n
                sem_val = evac_emit(gi, ps, pev)
                bank_free[bank] = sem_val
                proj_g[0] += 1

        def proj_pass(wslot, rhs_fn, evac_fn, tbs=range(NTB), ns_waits=None):
            use_slot(wslot)
            W = Wb[wslot % 2]
            cv = conv_done[wslot]
            if ns_waits is None:
                items = [(tb, m, ns) for tb in tbs for m in range(CH) for ns in range(2)]
            else:
                items = [(tb, m, ns) for tb in tbs for ns in range(2) for m in range(CH)]

            def mm(gi, ps):
                tb, m, ns = items[gi]
                cv[0].wait(TE, cv[1])
                if ns_waits is not None:
                    ns_waits[ns][0].wait(TE, ns_waits[ns][1])
                for kc in range(CH):
                    inst = TE.matmul(out=ps, lhsT=W[:, kc, m * 128:(m + 1) * 128],
                                     rhs=rhs_fn(kc, tb, ns),
                                     start=(kc == 0), stop=(kc == CH - 1))
                return inst

            def ev(gi, ps, pev):
                tb, m, ns = items[gi]
                return evac_fn(m, tb, ns, ps, pev)
            proj_groups(len(items), mm, ev)
            pass_pe_end[wslot] = gPE.n

        def evac_plain_act(dst3, scale=None):
            def f(m, tb, ns, ps, pev):
                gPE.wait(SE, pev)
                col = tb * 1024 + ns * 512
                if scale is None:
                    SE.copy(out=dst3[:, m, col:col + 512], in_=ps)
                else:
                    SE.activation(out=dst3[:, m, col:col + 512], in_=ps,
                                  func=AF.Copy, scale=scale)
                return (gSE, gSE.n)
            return f

        def evac_bias_dve(dst3, eidx, colmap=None, first_waits=()):
            fw = [list(first_waits)]

            def f(m, tb, ns, ps, pev):
                gPE.wait(VE, pev)
                for sem, val in fw[0]:
                    sem.wait(VE, val)
                fw[0] = []
                col = (tb * 1024 + ns * 512) if colmap is None else colmap(tb, ns)
                VE.tensor_scalar(out=dst3[:, m, col:col + 512], in0=ps,
                                 scalar1=be_fm[:, m, eidx:eidx + 1], scalar2=None,
                                 op0=OP.add, op1=OP.bypass)
                return (gEV, gEV.n)
            return f

        def attn_norms(pairs_q, heads_k, qh3, kh3, pair_kv, extra_waits=()):
            qcol, kcol, ccol = 40, 56, 72
            items = [("q", i, qh3, qcol + i) for i in pairs_q] + \
                    [("k", i, kh3, kcol + i) for i in heads_k]
            sq_marks = [None] * len(items)
            mm_marks = [None] * len(items)
            first = True
            for idx, (side, i, src3, cell) in enumerate(items):
                half = idx % 2
                sq = Dmf[:, half * 2048:half * 2048 + 2048]
                gPE.wait(VE)
                gSE.wait(VE)
                if first:
                    for sm, v in extra_waits:
                        sm.wait(VE, v)
                    first = False
                VE.tensor_tensor(out=sq, in0=src3[:, i, :], in1=src3[:, i, :],
                                 op=OP.mult)
                sq_marks[idx] = gEV.n
                if idx >= 1:
                    j = idx - 1
                    pj = (j % 2)
                    sqj = Dmf[:, pj * 2048:pj * 2048 + 2048]
                    gEV.wait(TE, sq_marks[j])
                    for ns in range(4):
                        bfree(2 + ns, TE)
                        inst = TE.matmul(out=psS[0:1, ns * 512:ns * 512 + 512],
                                         lhsT=on_b[:, 0:1],
                                         rhs=sqj[:, ns * 512:(ns + 1) * 512],
                                         start=True, stop=True)
                        gPE.inc(inst)
                    mm_marks[j] = gPE.n
                    gPE.wait(VE, mm_marks[j])
                    VE.tensor_scalar(out=LNwf[0:1, 0:2048], in0=psS[0:1, 0:2048],
                                     scalar1=1.0, scalar2=None, op0=OP.mult,
                                     op1=OP.max,
                                     accum_out=rowb[0:1, items[j][3]:items[j][3] + 1])
                    for ns in range(4):
                        bank_free[2 + ns] = (gEV, gEV.n)
            j = len(items) - 1
            pj = j % 2
            sqj = Dmf[:, pj * 2048:pj * 2048 + 2048]
            gEV.wait(TE, sq_marks[j])
            for ns in range(4):
                bfree(2 + ns, TE)
                inst = TE.matmul(out=psS[0:1, ns * 512:ns * 512 + 512],
                                 lhsT=on_b[:, 0:1], rhs=sqj[:, ns * 512:(ns + 1) * 512],
                                 start=True, stop=True)
                gPE.inc(inst)
            gPE.wait(VE, gPE.n)
            VE.tensor_scalar(out=LNwf[0:1, 0:2048], in0=psS[0:1, 0:2048],
                             scalar1=1.0, scalar2=None, op0=OP.mult, op1=OP.max,
                             accum_out=rowb[0:1, items[j][3]:items[j][3] + 1])
            for ns in range(4):
                bank_free[2 + ns] = (gEV, gEV.n)
            for pair in pairs_q:
                kvh = pair_kv(pair)
                VE.tensor_tensor(out=rowb[0:1, ccol + pair:ccol + pair + 1],
                                 in0=rowb[0:1, qcol + pair:qcol + pair + 1],
                                 in1=rowb[0:1, kcol + kvh:kcol + kvh + 1], op=OP.mult)
            npair = len(pairs_q)
            gEV.wait(SE)
            SE.activation(out=rowb[0:1, ccol:ccol + npair],
                          in_=rowb[0:1, ccol:ccol + npair], func=AF.Ln,
                          bias=sc_fm[0:1, 42:43])
            gSE.wait(SE, gSE.n)
            SE.activation(out=rowb[0:1, ccol:ccol + npair],
                          in_=rowb[0:1, ccol:ccol + npair], func=AF.Exp, scale=0.5)
            gSE.wait(VE)
            VE.tensor_scalar(out=rowb[0:1, ccol:ccol + npair],
                             in0=rowb[0:1, ccol:ccol + npair],
                             scalar1=-40.0, scalar2=0.0, op0=OP.add, op1=OP.max)
            gEV.wait(TE)
            bfree(6, TE)
            for pair in pairs_q:
                TE.matmul(out=psY[:, pair:pair + 1], lhsT=on_f[0:1, :],
                          rhs=rowb[0:1, ccol + pair:ccol + pair + 1],
                          start=(pair == pairs_q[0]), stop=(pair == pairs_q[-1]))
            gPE.inc(TE.last)
            gPE.wait(VE)
            VE.tensor_scalar(out=negC[:, 0:len(pairs_q)], in0=psY[:, 0:len(pairs_q)],
                             scalar1=-1.0, scalar2=None, op0=OP.mult, op1=OP.bypass)
            bank_free[6] = (gEV, gEV.n)

        def attn_unit(pair, h_kv, qb, qh3, kh3, vh_fn, ctx_dst, causal, negC_col,
                      first_exp_waits=(), rscale=None, prev_finish=None, r_bf=None,
                      par=0, prev_tail=None):
            n_kt = (qb * 4 + 4) if causal else KT
            ngrp = n_kt // 4
            exp_marks = []
            if par == 0:
                psY_u, ybank = psY[:, 0:512], 6
                psM_u, mbank = psM, 7
            else:
                psY_u, ybank = psA[:, 0:512], 0
                psM_u, mbank = psA[:, 512:1024], 1

            def emit_scores(g4):
                for j in range(4):
                    kt = g4 * 4 + j
                    bfree(2 + j, TE)
                    inst = TE.matmul(out=psS[:, j * 512:(j + 1) * 512],
                                     lhsT=kh3[:, h_kv, kt * 128:(kt + 1) * 128],
                                     rhs=qh3[:, pair, qb * 512:(qb + 1) * 512],
                                     start=True, stop=True)
                    gPE.inc(inst)
                pev = gPE.n
                if g4 == 0:
                    for sm, v in first_exp_waits:
                        sm.wait(SE, v)
                if causal and g4 == qb:
                    gPE.wait(VE, pev)
                    for j in range(4):
                        VE.tensor_tensor(out=psS[:, j * 512:(j + 1) * 512],
                                         in0=psS[:, j * 512:(j + 1) * 512],
                                         in1=mask_sb[:, j, :], op=OP.add)
                    gEV.wait(SE)
                else:
                    gPE.wait(SE, pev - 2)
                SE.activation(out=Dp[:, g4 * 4:g4 * 4 + 2, :].rearrange("p a b -> p (a b)"),
                              in_=psS[:, 0:1024], func=AF.Exp,
                              bias=negC[:, negC_col:negC_col + 1], scale=1.0)
                m0 = gSE.n
                if not (causal and g4 == qb):
                    gPE.wait(SE, pev)
                SE.activation(out=Dp[:, g4 * 4 + 2:g4 * 4 + 4, :].rearrange("p a b -> p (a b)"),
                              in_=psS[:, 1024:2048], func=AF.Exp,
                              bias=negC[:, negC_col:negC_col + 1], scale=1.0)
                exp_marks.append((m0, gSE.n))
                bank_free[2] = bank_free[3] = (gSE, m0)
                bank_free[4] = bank_free[5] = (gSE, gSE.n)

            pv_last = [None]

            def emit_pv_quad(g4):
                m0, m1 = exp_marks[g4]
                gSE.wait(TE, m0)
                if g4 == 0:
                    bfree(ybank, TE)
                    bfree(mbank, TE)
                for j in range(2):
                    kt = g4 * 4 + j
                    TE.matmul(out=psY_u, lhsT=vh_fn(kt), rhs=Dp[:, kt, :],
                              start=(kt == 0), stop=(kt == n_kt - 1))
                gSE.wait(TE, m1)
                for j in range(2, 4):
                    kt = g4 * 4 + j
                    TE.matmul(out=psY_u, lhsT=vh_fn(kt), rhs=Dp[:, kt, :],
                              start=(kt == 0), stop=(kt == n_kt - 1))
                for j in range(4):
                    kt = g4 * 4 + j
                    inst = TE.matmul(out=psM_u[0:1, 0:512], lhsT=on_b[:, 0:1],
                                     rhs=Dp[:, kt, :],
                                     start=(kt == 0), stop=(kt == n_kt - 1))
                pv_last[0] = inst

            if ngrp == 1 and prev_tail is not None:
                prev_tail()
            emit_scores(0)
            if ngrp > 1 and prev_tail is not None:
                prev_tail()
            if ngrp > 1:
                emit_scores(1)
            if prev_finish is not None:
                prev_finish()
            if ngrp > 1:
                emit_pv_quad(0)
                for g in range(2, ngrp):
                    emit_scores(g)
                    emit_pv_quad(g - 1)
            else:
                pass

            def tail():
                emit_pv_quad(ngrp - 1)
                gPE.inc(pv_last[0])
                pv_done = gPE.n
                gPE.wait(SE, pv_done)
                SE.activation(out=rowb[64:65, 512 * par:512 * par + 512],
                              in_=psM_u[0:1, 0:512], func=AF.Ln,
                              bias=sc_fm[0:1, 43:44])
                bank_free[mbank] = (gSE, gSE.n)
                gSE.wait(SE, gSE.n)
                SE.activation(out=rowb[64:65, 512 * par:512 * par + 512],
                              in_=rowb[64:65, 512 * par:512 * par + 512], func=AF.Exp,
                              scale=-1.0,
                              bias=(0.0 if rscale is None
                                    else sc_fm[64:65, rscale + 6:rscale + 7]))
                gSE.wait(VE)
                VE.tensor_copy(out=r_bf, in_=rowb[64:65, 512 * par:512 * par + 512])
                rec_done[0] = gEV.n

            rec_done = [0]

            def finish():
                gEV.wait(TE, rec_done[0])
                bfree(mbank, TE)
                TE.matmul(out=psM_u[:, 0:512], lhsT=on_b[0:1, :], rhs=r_bf,
                          start=True, stop=True)
                gPE.inc(TE.last)
                bc_done = gPE.n
                gPE.wait(VE, bc_done)
                VE.tensor_copy(out=rbc_sb, in_=psM_u[:, 0:512])
                bank_free[mbank] = (gEV, gEV.n)
                VE.tensor_tensor(out=ctx_dst, in0=psY_u, in1=rbc_sb, op=OP.mult)
                bank_free[ybank] = (gEV, gEV.n)
            return tail, finish

        def emit_ln(src, dst, eps, affine):
            gSE.wait(VE)
            gPE.wait(VE)
            dSf.wait(VE)
            dSb.wait(VE)
            dSn.wait(VE)
            cmarks, smarks = [], []
            for c in range(CH):
                VE.tensor_copy(out=Dn[:, c, :], in_=src[:, c, :])
                cmarks.append(gEV.n)
                gEV.wait(SE, cmarks[-1])
                SE.activation(out=Dsq[:, c, :], in_=Dn[:, c, :], func=AF.Square)
                smarks.append(gSE.n)
            cst = cmarks[-1]
            sqv = smarks[-1]
            bfree(7, TE)
            bfree(6, TE)
            for c in range(CH):
                gEV.wait(TE, cmarks[c])
                gSE.wait(TE, smarks[c])
                i1 = TE.matmul(out=psM[0:1, 0:512], lhsT=on_b[:, 0:1], rhs=Dn[:, c, :],
                               start=(c == 0), stop=(c == CH - 1))
                i2 = TE.matmul(out=psY[0:1, 0:512], lhsT=on_b[:, 0:1], rhs=Dsq[:, c, :],
                               start=(c == 0), stop=(c == CH - 1))
            gPE.inc(i2)
            st_done = gPE.n
            gPE.wait(VE, st_done)
            VE.tensor_scalar(out=rowb[0:1, 0:512], in0=psM[0:1, 0:512],
                             scalar1=1.0 / 1024, scalar2=None, op0=OP.mult, op1=OP.bypass)
            VE.tensor_scalar(out=rowb[0:1, 512:1024], in0=psY[0:1, 0:512],
                             scalar1=1.0 / 1024, scalar2=None, op0=OP.mult, op1=OP.bypass)
            VE.tensor_tensor(out=x1wf[0:1, 2048:2560], in0=rowb[0:1, 0:512],
                             in1=rowb[0:1, 0:512], op=OP.mult)
            VE.tensor_tensor(out=rowb[0:1, 512:1024], in0=rowb[0:1, 512:1024],
                             in1=x1wf[0:1, 2048:2560], op=OP.subtract)
            VE.tensor_scalar(out=rowb[0:1, 512:1024], in0=rowb[0:1, 512:1024],
                             scalar1=eps, scalar2=None, op0=OP.add, op1=OP.bypass)
            gEV.wait(SE)
            SE.activation(out=rowb[0:1, 512:1024], in_=rowb[0:1, 512:1024], func=AF.Ln)
            gSE.wait(SE, gSE.n)
            SE.activation(out=rowb[0:1, 512:1024], in_=rowb[0:1, 512:1024], func=AF.Exp,
                          scale=-0.5)
            fin = gSE.n
            gSE.wait(TE, fin)
            TE.matmul(out=psM[:, 0:512], lhsT=on_f[0:1, :], rhs=rowb[0:1, 0:512],
                      start=True, stop=True)
            TE.matmul(out=psY[:, 0:512], lhsT=on_f[0:1, :], rhs=rowb[0:1, 512:1024],
                      start=True, stop=True)
            gPE.inc(TE.last)
            bcv = gPE.n
            gPE.wait(VE, bcv)
            for c in range(CH):
                VE.tensor_tensor(out=src[:, c, :], in0=src[:, c, :], in1=psM[:, 0:512],
                                 op=OP.subtract)
            for c in range(CH):
                VE.tensor_tensor(out=dst[:, c, :], in0=src[:, c, :], in1=psY[:, 0:512],
                                 op=OP.mult)
                if affine:
                    VE.tensor_scalar(out=dst[:, c, :], in0=dst[:, c, :],
                                     scalar1=gb_fm[:, c, 0:1], scalar2=gb_fm[:, c, 1:2],
                                     op0=OP.mult, op1=OP.add)
                bank_free[7] = (gEV, gEV.n)
            bank_free[6] = (gEV, gEV.n)
            return cst

        stage_free = [None, None]
        for tt in range(KT):
            sl = tt % 2
            stg = x1wf[:, sl * 1024:(sl + 1) * 1024]
            if stage_free[sl] is not None:
                s, v = stage_free[sl]
                s.wait(SE, v)
            SE.dma_start(out=stg, in_=x_in[tt * 128:(tt + 1) * 128, :]).then_inc(dX[sl].h, 16)
            dX[sl].n += 16
            dX[sl].wait(TE)
            for half, pbank, key in ((0, psM, 7), (1, psY, 6)):
                bfree(key, TE)
                for j in range(4):
                    c = half * 4 + j
                    inst = TE.transpose(out=pbank[:, j * 128:(j + 1) * 128],
                                        in_=stg[:, c * 128:(c + 1) * 128],
                                        identity=id_f[:])
                gPE.inc(inst)
                gPE.wait(VE)
                VE.tensor_copy(out=Dm3[:, half * 4:half * 4 + 4, tt * 128:(tt + 1) * 128],
                               in_=pbank[:, 0:512].rearrange("p (a b) -> p a b", a=4))
                bank_free[key] = (gEV, gEV.n)
            stage_free[sl] = (gPE, gPE.n)

        if upto == "load":
            return dump_exit(Dm3) or nc
        if upto == "w0":
            use_slot(0)
            gEV.wait(GP)
            GP.dma_start(out=fmv(dbg_t)[:, :, 0:1024],
                         in_=Wb0.ap().rearrange("p c n -> p c n")).then_inc(dO.h, 16)
            dO.n += 16
            dO.wait(GP, dO.n)
            finish_all()
            return nc
        gEV.wait(TE)
        proj_pass(0, lambda kc, tb, ns: Dm3[:, kc, tb * 1024 + ns * 512:tb * 1024 + ns * 512 + 512],
                  evac_plain_act(A3, scale=RS))

        if upto == "qh":
            return dump_exit(A3) or nc
        use_slot(1)
        Wkv = Wb[1]
        cv1 = conv_done[1]
        items3 = [(h, kv, tb, ns) for tb in range(NTB) for kv in range(2)
                  for h in range(2) for ns in range(2)]

        def mm_kv(gi, ps):
            h, kv, tb, ns = items3[gi]
            cv1[0].wait(TE, cv1[1])
            col = kv * 64 + h * 32
            for kc in range(CH):
                inst = TE.matmul(out=ps[0:32, :], lhsT=Wkv[:, kc, col:col + 32],
                                 rhs=Dm3[:, kc, tb * 1024 + ns * 512:tb * 1024 + ns * 512 + 512],
                                 start=(kc == 0), stop=(kc == CH - 1))
            return inst

        def ev_kv(gi, ps, pev):
            h, kv, tb, ns = items3[gi]
            dst = klat if kv == 0 else vlat
            gPE.wait(VE, pev)
            VE.tensor_copy(out=dst[:, h, tb * 1024 + ns * 512:tb * 1024 + ns * 512 + 512],
                           in_=ps[0:32, :])
            return (gEV, gEV.n)
        proj_groups(len(items3), mm_kv, ev_kv)
        pass_pe_end[1] = gPE.n

        if upto == "lat":
            return dump_exit(B3) or nc
        gEV.wait(TE, wkup_ready)
        gSE.wait(TE)
        for h in range(2):
            for ns in range(4):
                ps, bank = proj_bank()
                bfree(bank, TE)
                inst = TE.matmul(out=ps, lhsT=wkup[:, 0, :],
                                 rhs=klat[:, h, ns * 512:(ns + 1) * 512],
                                 start=True, stop=True)
                gPE.inc(inst)
                gPE.wait(VE)
                VE.tensor_copy(out=khA[:, h, ns * 512:(ns + 1) * 512], in_=ps)
                bank_free[bank] = (gEV, gEV.n)
                proj_g[0] += 1
        for h in range(2):
            for q4 in range(4):
                pbank, key = (psM, 7) if q4 % 2 == 0 else (psY, 6)
                bfree(key, TE)
                for j in range(4):
                    kt = q4 * 4 + j
                    inst = TE.matmul(out=pbank[:, j * 128:(j + 1) * 128],
                                     lhsT=vlat[:, h, kt * 128:(kt + 1) * 128],
                                     rhs=wkup[:, 1, :], start=(j == 0), stop=(j == 3))
                gPE.inc(inst)
                gPE.wait(VE)
                VE.tensor_copy(out=Cv[:, q4 * 4:q4 * 4 + 4, h * 128:h * 128 + 128],
                               in_=pbank[:, 0:512].rearrange("p (a b) -> p a b", a=4))
                bank_free[key] = (gEV, gEV.n)

        if upto == "kv":
            return dump_exit(B3) or nc
        if upto == "vh":
            return dump_exit(Cv.ap().rearrange("p a b -> p (a b)").rearrange("p (c t) -> p c t", c=CH)) or nc
        attn_norms(range(8), range(2), A3, khA, lambda p: p >> 2)
        if upto == "norm":
            gEV.wait(GP)
            GP.dma_start(out=dbg_t[0:128, 0:8], in_=negC[:]).then_inc(dO.h, 16)
            dO.n += 16
            dO.wait(GP, dO.n)
            finish_all()
            return nc
        use_slot(2)
        Wo_A = Wb[0]
        cvo = conv_done[2]
        gSE.wait(TE)
        gEV.wait(TE)
        x1_written = 0
        nstage_dma = 0
        ln_cst = [0]
        for qb in range(NQB):
            few = [(gEV, ln_cst[0])] if qb > 0 else []
            fins = []
            tails = []
            for pair in range(8):
                kvh = pair >> 2
                pf = fins[-2] if len(fins) >= 2 else None
                pt = tails[-1] if tails else None
                t, f = attn_unit(pair, kvh, qb, A3, khA,
                                 lambda kt, kvh=kvh: Cv[:, kt, kvh * 128:kvh * 128 + 128],
                                 Dc[:, pair, :], True, pair,
                                 first_exp_waits=(few if pair == 0 else []),
                                 prev_finish=pf, r_bf=Dq[0:1, pair % 2, 0:512],
                                 par=pair % 2, prev_tail=pt)
                tails.append(t)
                fins.append(f)
            tails[-1]()
            fins[-2]()
            fins[-1]()
            for tt in range(4):
                sl = tt % 2
                stg = x1wf[:, sl * 1024:(sl + 1) * 1024]
                if stage_free[sl] is not None:
                    s, v = stage_free[sl]
                    s.wait(SE, v)
                gEV.wait(SE)
                dSb.wait(SE)
                dSf.wait(SE)
                SE.dma_start(out=stg, in_=x_in[qb * 512 + tt * 128: qb * 512 + (tt + 1) * 128, :]
                             ).then_inc(dX[sl].h, 16)
                dX[sl].n += 16
                dX[sl].wait(TE)
                for half, pbank, key in ((0, psM, 7), (1, psY, 6)):
                    bfree(key, TE)
                    for j in range(4):
                        c = half * 4 + j
                        inst = TE.transpose(out=pbank[:, j * 128:(j + 1) * 128],
                                            in_=stg[:, c * 128:(c + 1) * 128],
                                            identity=id_f[:])
                    gPE.inc(inst)
                    gPE.wait(VE)
                    VE.tensor_copy(out=LNw[:, half * 4:half * 4 + 4, tt * 128:(tt + 1) * 128],
                                   in_=pbank[:, 0:512].rearrange("p (a b) -> p a b", a=4))
                    bank_free[key] = (gEV, gEV.n)
                stage_free[sl] = (gPE, gPE.n)

            gEV.wait(TE)

            def mm_o(gi, ps):
                cvo[0].wait(TE, cvo[1])
                for kc in range(CH):
                    inst = TE.matmul(out=ps, lhsT=Wo_A[:, kc, gi * 128:(gi + 1) * 128],
                                     rhs=Dc[:, kc, :], start=(kc == 0), stop=(kc == CH - 1))
                return inst

            def ev_o(gi, ps, pev):
                gPE.wait(VE, pev)
                VE.tensor_tensor(out=LNw[:, gi, :], in0=LNw[:, gi, :], in1=ps, op=OP.add)
                return (gEV, gEV.n)
            proj_groups(CH, mm_o, ev_o)
            ln_cst[0] = emit_ln(LNw, x1w, EPS_A, False)
            wb = gEV.n
            gEV.wait(GP, wb)
            GP.dma_start(out=fmv(x1b_t)[:, :, qb * 512:(qb + 1) * 512],
                         in_=x1w[:, :, :]).then_inc(dSb.h, 16)
            dSb.n += 16
            gEV.wait(SY, wb)
            SY.dma_start(out=fmv(x1f_t)[:, :, qb * 512:(qb + 1) * 512],
                         in_=x1w[:, :, :]).then_inc(dSf.h, 16)
            dSf.n += 16
        pass_pe_end[2] = gPE.n
        a_free[0] = gPE.n

        if upto == "A":
            if dbg_t is not None:
                dSb.wait(SY)
                SY.dma_start(out=dbg_t, in_=x1b_t).then_inc(dO.h, 16)
                dO.n += 16
            finish_all()
            return nc

        node_write_dma = {-1: (dSb, dSb.n)}

        def edge_pass_full(wslot, src_dram, eidx, src_ready, first_waits=(),
                           prev_read=None):
            evac = evac_bias_dve(Dm3, eidx, first_waits=first_waits)
            pr = a_free[0] if prev_read is None else prev_read
            for tb in range(NTB):
                src_ready[0].wait(SE, src_ready[1])
                gPE.wait(SE, pr)
                SE.dma_start(out=A3[:, :, (tb % 2) * 1024:(tb % 2) * 1024 + 512],
                             in_=fmv(src_dram)[:, :, tb * 1024:tb * 1024 + 512]
                             ).then_inc(dEa[tb % 2].h, 16)
                dEa[tb % 2].n += 16
                SE.dma_start(out=A3[:, :, (tb % 2) * 1024 + 512:(tb % 2) * 1024 + 1024],
                             in_=fmv(src_dram)[:, :, tb * 1024 + 512:(tb + 1) * 1024]
                             ).then_inc(dEb[tb % 2].h, 16)
                dEb[tb % 2].n += 16
                proj_pass(wslot,
                          lambda kc, tb_, ns: A3[:, kc, (tb % 2) * 1024 + ns * 512:(tb % 2) * 1024 + ns * 512 + 512],
                          evac, tbs=[tb],
                          ns_waits=[(dEa[tb % 2], dEa[tb % 2].n),
                                    (dEb[tb % 2], dEb[tb % 2].n)])

        for c, (eq, inq, ek, ink, ev, inv) in enumerate(NODES):
            ords = list(slot_order[c])
            share_all = (c == 0)
            share_qv = (eq == ev and inq == inv and not share_all)
            si = 0
            e_first = eq if (share_all or share_qv) else ev
            in_first = inq if (share_all or share_qv) else inv
            gEV.wait(GP)
            edge_pass_full(ords[si], node_src[in_first], e_first,
                           node_write_dma[in_first],
                           first_waits=[(dSn, dSn.n), (dSq, dSq.n), (dSb, dSb.n)])
            si += 1
            use_slot(ords[si])
            Wv_s = Wb[ords[si] % 2]
            cvv = conv_done[ords[si]]
            gEV.wait(TE)
            gSE.wait(TE)
            for kt in range(KT):
                for ns in range(2):
                    ps, bank = proj_bank()
                    bfree(bank, TE)
                    cvv[0].wait(TE, cvv[1])
                    for kc in range(CH):
                        inst = TE.matmul(out=ps, lhsT=Dm3[:, kc, kt * 128:(kt + 1) * 128],
                                         rhs=Wv_s[:, kc, ns * 512:(ns + 1) * 512],
                                         start=(kc == 0), stop=(kc == CH - 1))
                    gPE.inc(inst)
                    gPE.wait(SE)
                    SE.copy(out=Cv[:, kt, ns * 512:(ns + 1) * 512], in_=ps)
                    bank_free[bank] = (gSE, gSE.n)
                    proj_g[0] += 1
            pass_pe_end[ords[si]] = gPE.n
            si += 1
            if not (share_all or share_qv):
                gSE.wait(GP)
                edge_pass_full(ords[si], node_src[inq], eq, node_write_dma[inq],
                               first_waits=[(dSq, dSq.n)],
                               prev_read=pass_pe_end[ords[0]])
                si += 1
            gEV.wait(TE)
            proj_pass(ords[si],
                      lambda kc, tb, ns: Dm3[:, kc, tb * 1024 + ns * 512:tb * 1024 + ns * 512 + 512],
                      evac_plain_act(A3, scale=sc_fm[:, 24 + c:25 + c])); si += 1
            gEV.wait(SY)
            SY.dma_start(out=fmv(qsp_t), in_=Dm3[:, :, :]).then_inc(dSq.h, 16)
            dSq.n += 16
            qsp_dma = dSq.n
            if share_all:
                proj_pass(ords[si],
                          lambda kc, tb, ns: Dm3[:, kc, tb * 1024 + ns * 512:tb * 1024 + ns * 512 + 512],
                          evac_plain_act(B3, scale=sc_fm[:, 3 * c + 1:3 * c + 2])); si += 1
            else:
                ek_slot, kh_slot = ords[si], ords[si + 1]
                si += 2
                for tb in range(NTB):
                    node_write_dma[ink][0].wait(SE, node_write_dma[ink][1])
                    gPE.wait(SE, a_free[0])
                    SE.dma_start(out=B3[:, :, (tb % 2) * 1024:(tb % 2) * 1024 + 512],
                                 in_=fmv(node_src[ink])[:, :, tb * 1024:tb * 1024 + 512]
                                 ).then_inc(dNk[tb % 2].h, 16)
                    dNk[tb % 2].n += 16
                    SE.dma_start(out=B3[:, :, (tb % 2) * 1024 + 512:(tb % 2) * 1024 + 1024],
                                 in_=fmv(node_src[ink])[:, :, tb * 1024 + 512:(tb + 1) * 1024]
                                 ).then_inc(dNkb[tb % 2].h, 16)
                    dNkb[tb % 2].n += 16
                    first_ev = [True]

                    def evac_ek(m, tb_, ns, ps, pev):
                        gPE.wait(VE, pev)
                        if first_ev[0]:
                            dSq.wait(VE, qsp_dma)
                            first_ev[0] = False
                        col = tb_ * 1024 + ns * 512
                        VE.tensor_scalar(out=Dm3[:, m, col:col + 512], in0=ps,
                                         scalar1=be_fm[:, m, ek:ek + 1], scalar2=None,
                                         op0=OP.add, op1=OP.bypass)
                        return (gEV, gEV.n)
                    proj_pass(ek_slot,
                              lambda kc, tb_, ns: B3[:, kc, (tb_ % 2) * 1024 + ns * 512:(tb_ % 2) * 1024 + ns * 512 + 512],
                              evac_ek, tbs=[tb],
                              ns_waits=[(dNk[tb % 2], dNk[tb % 2].n),
                                        (dNkb[tb % 2], dNkb[tb % 2].n)])
                gEV.wait(TE)
                proj_pass(kh_slot,
                          lambda kc, tb, ns: Dm3[:, kc, tb * 1024 + ns * 512:tb * 1024 + ns * 512 + 512],
                          evac_plain_act(B3, scale=sc_fm[:, 3 * c + 1:3 * c + 2]))
            attn_norms(range(8), range(8), A3, B3, lambda p: p,
                       extra_waits=[(dSq, qsp_dma)])
            wo_slot = ords[si]; si += 1
            use_slot(wo_slot)
            Wo_s = Wb[wo_slot % 2]
            cvo2 = conv_done[wo_slot]
            is_last = (c == 5)
            attn_pe_end = [0]
            gSE.wait(TE)
            gEV.wait(TE)
            for qb in range(NQB):
                dSq.wait(SE, qsp_dma)
                gEV.wait(SE)
                gPE.wait(SE)
                SE.dma_start(out=Dq, in_=fmv(qsp_t)[:, :, qb * 512:(qb + 1) * 512]
                             ).then_inc(dQr.h, 16)
                dQr.n += 16
                qrl = dQr.n
                few = [(gEV, gEV.n)] if (is_last and qb > 0) else []
                fins = []
                tails = []
                for h in range(8):
                    pf = fins[-2] if len(fins) >= 2 else None
                    pt = tails[-1] if tails else None
                    t, f = attn_unit(h, h, qb, A3, B3,
                                     lambda kt, h=h: Cv[:, kt, h * 128:(h + 1) * 128],
                                     Dc[:, h, :], False, h,
                                     first_exp_waits=(few if h == 0 else []),
                                     rscale=30 + c, prev_finish=pf,
                                     r_bf=mask_sb[0:1, h % 2, :], par=h % 2,
                                     prev_tail=pt)
                    tails.append(t)
                    fins.append(f)
                tails[-1]()
                fins[-2]()
                fins[-1]()
                if qb == NQB - 1:
                    attn_pe_end[0] = gPE.n
                gEV.wait(TE)

                def mm_o2(gi, ps):
                    cvo2[0].wait(TE, cvo2[1])
                    for kc in range(CH):
                        inst = TE.matmul(out=ps, lhsT=Wo_s[:, kc, gi * 128:(gi + 1) * 128],
                                         rhs=Dc[:, kc, :], start=(kc == 0), stop=(kc == CH - 1))
                    return inst

                def ev_o2(gi, ps, pev):
                    gPE.wait(VE, pev)
                    dQr.wait(VE, qrl)
                    if is_last:
                        dO.wait(VE)
                    else:
                        dSn.wait(VE)
                        dSf.wait(VE)
                        dSb.wait(VE)
                    dst = LNw[:, gi, :] if is_last else x1w[:, gi, :]
                    VE.scalar_tensor_tensor(out=dst, in0=Dq[:, gi, :],
                                            scalar=sc_fm[:, 3 * c:3 * c + 1], in1=ps,
                                            op0=OP.mult, op1=OP.add)
                    return (gEV, gEV.n)
                proj_groups(CH, mm_o2, ev_o2)
                if not is_last:
                    nwb = gEV.n
                    gEV.wait(GP, nwb)
                    GP.dma_start(out=fmv(nsc_t[c])[:, :, qb * 512:(qb + 1) * 512],
                                 in_=x1w[:, :, :]).then_inc(dSn.h, 16)
                    dSn.n += 16
                else:
                    emit_ln(LNw, LNw, EPS_F, True)
                    dSf.wait(SE)
                    dSn.wait(SE)
                    gEV.wait(SE)
                    gPE.wait(SE)
                    SE.dma_start(out=x1w[:, :, :],
                                 in_=fmv(x1f_t)[:, :, qb * 512:(qb + 1) * 512]
                                 ).then_inc(dXf.h, 16)
                    dXf.n += 16
                    x1r = dXf.n
                    dXf.wait(VE, x1r)
                    for cc in range(CH):
                        VE.tensor_tensor(out=LNw[:, cc, :], in0=LNw[:, cc, :],
                                         in1=x1w[:, cc, :], op=OP.add)
                    emit_ln(LNw, x1w, EPS_A, False)
                    for tt in range(4):
                        for half, pbank, key in ((0, psM, 7), (1, psY, 6)):
                            bfree(key, TE)
                            gEV.wait(TE)
                            for j in range(4):
                                cc = half * 4 + j
                                inst = TE.transpose(out=pbank[:, j * 128:(j + 1) * 128],
                                                    in_=x1w[:, cc, tt * 128:(tt + 1) * 128],
                                                    identity=id_f[:])
                            gPE.inc(inst)
                            gPE.wait(VE)
                            dO.wait(VE)
                            VE.tensor_copy(
                                out=LNwf[:, half * 512:(half + 1) * 512].rearrange(
                                    "p (a b) -> p a b", a=4),
                                in_=pbank[:, 0:512].rearrange("p (a b) -> p a b", a=4))
                            bank_free[key] = (gEV, gEV.n)
                        wrt = gEV.n
                        gEV.wait(SY, wrt)
                        SY.dma_start(out=out_t[qb * 512 + tt * 128: qb * 512 + (tt + 1) * 128, :],
                                     in_=LNwf[:, 0:1024]).then_inc(dO.h, 16)
                        dO.n += 16
            pass_pe_end[wo_slot] = gPE.n
            a_free[0] = attn_pe_end[0]
            node_write_dma[c] = (dSn, dSn.n)
            if dbg_t is not None and dbg_node == c and c < 5:
                dSn.wait(SY)
                SY.dma_start(out=dbg_t, in_=nsc_t[c]).then_inc(dO.h, 16)
                dO.n += 16
            if upto == c:
                finish_all()
                return nc

        finish_all()
    return nc


_NC_CACHE = {}


def _get_nc(upto="full", dbg_node=None):
    key = (upto, dbg_node)
    if key not in _NC_CACHE:
        _NC_CACHE[key] = build_nc(upto, dbg_node)
    return _NC_CACHE[key]


def make_consts():
    bf = ml_dtypes.bfloat16
    ident = np.eye(128, dtype=np.float32)
    ones_f = np.ones((128, 128), dtype=np.float32)
    ones_b = np.ones((128, 128), dtype=bf)
    p = np.arange(128)[:, None, None]
    si = np.arange(4)[None, :, None]
    j = np.arange(512)[None, None, :]
    mask = np.where(j - si * 128 - p >= 0, 0.0, -1e30).astype(bf)
    return {"ident_f": ident, "ones_f": ones_f, "ones_b": ones_b, "mask": mask}


def make_inmaps(inputs):
    consts = make_consts()
    shared = {
        "attn_wq": inputs["L0_attn_wq"], "attn_wk_down": inputs["L0_attn_wk_down"],
        "attn_wv_down": inputs["L0_attn_wv_down"], "attn_wk_up": inputs["L0_attn_wk_up"],
        "attn_wv_up": inputs["L0_attn_wv_up"], "attn_wo": inputs["L0_attn_wo"],
        "ffn_ew": inputs["L0_ffn_ew"], "ffn_act_w": inputs["L0_ffn_act_w"],
        "ffn_We": inputs["L0_ffn_We"], "ffn_be": inputs["L0_ffn_be"],
        "ffn_Wq": inputs["L0_ffn_Wq"], "ffn_Wk": inputs["L0_ffn_Wk"],
        "ffn_Wv": inputs["L0_ffn_Wv"], "ffn_Wo": inputs["L0_ffn_Wo"],
        "ffn_ln_g": inputs["L0_ffn_ln_g"], "ffn_ln_b": inputs["L0_ffn_ln_b"],
    }
    shared = {k: np.ascontiguousarray(np.asarray(v), dtype=np.float32)
              for k, v in shared.items()}
    shared.update(consts)
    x = np.asarray(inputs["L0_attn_x"])
    maps = []
    for b in range(8):
        m = dict(shared)
        m["x"] = np.ascontiguousarray(x[b], dtype=np.float32)
        maps.append(m)
    return maps


def kernel(**inputs):
    nc = _get_nc("full", None)
    maps = make_inmaps(inputs)
    res = run_bass_kernel_spmd(nc, maps, core_ids=list(range(8)))
    return np.stack([np.asarray(res.results[i]["out"]) for i in range(8)], axis=0)
